# version 16
# baseline (speedup 1.0000x reference)
import numpy as np
import concourse.bass as bass
import concourse.mybir as mybir
from concourse.bass_utils import run_bass_kernel_spmd

F32 = mybir.dt.float32
BF16 = mybir.dt.bfloat16
AF = mybir.ActivationFunctionType
ALU = mybir.AluOpType
AX = mybir.AxisListType

P = 128
NT = 16
D = 1024
NG = 8
EPS_LN = 1e-5
EPS_Z = 1e-9

DMA_SEMS = {"dX0", "dX1", "dX2", "dWe", "dOut", "dWc", "dWo", "dWb2", "dWf", "dWs", "dWr",
            "dWq0", "dWq1", "dWq2", "dWq3", "dWq4", "dWq5", "dWq6", "dWq7"}


class Builder:

    def __init__(self):
        self.sched = []
        self.count = {}

    def op(self, eng, fn, waits=(), inc=None):
        if eng in ("V", "A"):
            assert inc in (None, eng)
            inc = eng
        w = []
        for t in waits:
            if t is None:
                continue
            s, v = t
            assert v <= self.count.get(s, 0), f"wait on future: {s} {v}>{self.count.get(s,0)}"
            w.append((s, v))
        tok = None
        if inc is not None:
            amt = 16 if inc in DMA_SEMS else 1
            self.count[inc] = self.count.get(inc, 0) + amt
            tok = (inc, self.count[inc])
        self.sched.append((eng, w, fn, inc))
        return tok

    def replay(self, blk, sems):
        for eng_key, eng_name in [("S", "sync"), ("G", "gpsimd"), ("V", "vector"),
                                  ("A", "scalar"), ("T", "tensor")]:
            ops = [o for o in self.sched if o[0] == eng_key]
            serial = eng_key in ("V", "A", "G")

            def body(eng, ops=ops, eng_key=eng_key, serial=serial):
                last = {}
                self_idx = 0
                for (_e, waits, fn, inc) in ops:
                    if serial and self_idx > 0:
                        eng.wait_ge(sems[eng_key], self_idx)
                    for (s, v) in waits:
                        if s == eng_key:
                            continue
                        if last.get(s, -1) >= v:
                            continue
                        eng.wait_ge(sems[s], v)
                        last[s] = v
                    inst = fn(eng)
                    if inc is not None:
                        if inc == "cc":
                            inst.then_inc(sems[inc])
                        else:
                            inst.then_inc(sems[inc], 16 if inc in DMA_SEMS else 1)
                    if inc == eng_key:
                        self_idx += 1
            getattr(blk, eng_name)(body)


def build_nc(limit="all"):
    nc = bass.Bass("TRN2", target_bir_lowering=False, debug=False)

    x_hbm = nc.dram_tensor("x", [2048, D], F32, kind="ExternalInput").ap()
    wqkvT = nc.dram_tensor("wqkvT", [D, 3072], F32, kind="ExternalInput").ap()
    woutT = nc.dram_tensor("woutT", [D, D], F32, kind="ExternalInput").ap()
    baseT = nc.dram_tensor("baseT", [D, D], F32, kind="ExternalInput").ap()
    rw_h = nc.dram_tensor("rw", [D, 16], F32, kind="ExternalInput").ap()
    lraB = nc.dram_tensor("loraB", [64, D], F32, kind="ExternalInput").ap()
    rb_h = nc.dram_tensor("rb", [1, 8], F32, kind="ExternalInput").ap()
    bb_h = nc.dram_tensor("bb", [1, D], F32, kind="ExternalInput").ap()
    ident_h = nc.dram_tensor("ident", [P, P], F32, kind="ExternalInput").ap()
    maskM_h = nc.dram_tensor("maskM", [P, P], F32, kind="ExternalInput").ap()
    maskM2_h = nc.dram_tensor("maskM2", [P, 256], F32, kind="ExternalInput").ap()
    tri16_h = nc.dram_tensor("tri16", [16, 17], F32, kind="ExternalInput").ap()
    ones_h = nc.dram_tensor("onescol", [P, 1], F32, kind="ExternalInput").ap()
    smask_h = nc.dram_tensor("smask", [P, 1], F32, kind="ExternalInput").ap()
    rmask_h = nc.dram_tensor("rmask", [P, 1], F32, kind="ExternalInput").ap()
    cvals_h = nc.dram_tensor("cvals", [P, 2], F32, kind="ExternalInput").ap()
    o_hbm = nc.dram_tensor("out", [2048, D], F32, kind="ExternalOutput").ap()
    cc_in = nc.dram_tensor("cc_in", [P, 520], F32)
    cc_out = nc.dram_tensor("cc_out", [P, 520], F32)

    def alias(name, shape, dtype, base_handle, byte_off=0):
        h = nc.virtual_sbuf_tensor(name, shape, dtype)
        nc._tensor(h.name, list(h.shape), h.dtype, type="SB")
        mloc = nc.lookup_mloc(h)
        mloc.allocated = True
        mloc.addr = nc.lookup_mloc(base_handle).addr + byte_off
        return h

    alloc = nc.alloc_sbuf_tensor
    ident = alloc("identsb", [P, P], BF16)
    maskM = alloc("maskMsb", [P, P], BF16)
    maskM2 = alloc("maskM2sb", [P, 256], F32)
    tri16 = alloc("tri16sb", [16, 17], BF16)
    onescol = alloc("onescolsb", [P, 1], BF16)
    smask = alloc("smasksb", [P, 1], F32)
    rmask = alloc("rmasksb", [P, 1], F32)
    rb_bc = alloc("rb_bc", [P, 8], F32)
    cvals = alloc("cvalssb", [P, 2], F32)
    nc.const_aps.register(F32, EPS_LN, cvals[:, 0:1])
    nc.const_aps.register(F32, -0.5, cvals[:, 1:2])
    wreg = alloc("wreg", [P, 24576], BF16)
    qsm = alloc("qsm", [P, NT * D], BF16)
    ktm = alloc("ktm", [P, NT * D], BF16)
    vtm = alloc("vtm", [P, NT * D], BF16)
    xo = alloc("xo", [P, NG * 2048], BF16)
    zpref = alloc("zpref", [P, NG * 17], F32)
    srun = alloc("srun", [P, 512], F32)
    xin0 = alloc("xin0", [P, D], F32)
    xin1 = alloc("xin1", [P, D], F32)
    xhat0 = alloc("xhat0", [P, D], BF16)
    xhat1 = alloc("xhat1", [P, D], BF16)
    strip0 = alloc("strip0", [P, D], BF16)
    strip1 = alloc("strip1", [P, D], BF16)
    junk = alloc("junk", [P, D], F32)
    qe = alloc("qe", [P, D], F32)
    cols0 = alloc("cols0", [P, 8], F32)
    cols1 = alloc("cols1", [P, 8], F32)
    sseg = alloc("sseg", [P, 16], F32)
    rqb = alloc("rqbuf", [P, 16], F32)
    cz_sb = alloc("cz_sb", [P, 16], BF16)
    csums = alloc("csums", [P, P], F32)
    czT_sb = alloc("czT_sb", [16, P], BF16)

    xhat = [xhat0, xhat1]
    strip = [strip0, strip1]
    colsv = [cols0, cols1]

    x2a = alias("x2a", [P, 8 * D], F32, qsm)
    x2b = alias("x2b", [P, 8 * D], F32, ktm)
    wout = alias("wout", [P, 8192], BF16, wreg, 0)
    wbase = alias("wbase", [P, 8192], BF16, wreg, 16384)
    spref = alias("spref", [P, 8192], BF16, wreg, 32768)
    lorab_sb = alias("lorabsb", [64, D], BF16, wreg, 32768)
    rw_sb = alias("rw_sb", [P, 128], BF16, wreg, 32768 + 2048)
    bb_bc = alias("bb_bc", [P, D], F32, wreg, 32768 + 2048 + 256)
    t1 = alias("t1", [P, D], F32, junk)
    qe2 = alias("qe2", [P, D], F32, xo, 0)
    xin2 = alias("xin2", [P, D], F32, xo, 4096)
    spref_tmp = alias("spref_tmp", [P, 8192], BF16, xo, 8192)

    xin = [xin0, xin1, xin2]
    blob_s = alias("blob_s", [P, 520], F32, xhat0)
    brecv = alias("brecv", [P, 520], F32, strip0)
    qt = [alias(f"qt{j}", [P, 256], BF16, xin0, j * 512) for j in range(2)]
    kc = [alias(f"kc{j}", [P, 256], BF16, xin0, 1024 + j * 512) for j in range(2)]
    ab = [alias(f"ab{j}", [P, 512], BF16, xin0, 2048 + j * 1024) for j in range(2)]
    rt = [alias(f"rt{j}", [P, 256], F32, xin1, j * 1024) for j in range(2)]
    tmpO = [alias(f"tmpO{j}", [P, 256], F32, xin1, 2048 + j * 1024) for j in range(2)]
    a_sb0 = alias("a_sb0", [P, D], F32, vtm, 0)
    a_sb1 = alias("a_sb1", [P, D], F32, vtm, 4096)
    a_sb2 = alias("a_sb2", [P, D], F32, vtm, 20480)
    cols4v = [alias(f"cols4{j}", [P, 8], F32, vtm, 24576 + j * 32) for j in range(3)]
    cbuf = [alias(f"cbuf{j}", [P, 64], BF16, vtm, 8192 + j * 128) for j in range(2)]
    ctsb = [alias(f"ctsb{j}", [64, P], BF16, vtm, 8448 + j * 256) for j in range(2)]
    lgb = alias("lgb", [P, 8], F32, vtm, 8960)
    geb = alias("geb", [P, 8], F32, vtm, 8992)
    gsb = alias("gsb", [P, 1], F32, vtm, 9024)
    gqb = alias("gqb", [P, 1], F32, vtm, 9056)
    gbuf = alias("gbuf", [P, 8], BF16, vtm, 9088)
    gbuf32 = alias("gbuf32", [P, 8], F32, vtm, 9120)
    junk2 = alias("junk2", [P, D], F32, vtm, 12288)
    cols3v = [alias(f"cols3{j}", [P, 8], F32, vtm, 16384 + j * 32) for j in range(2)]
    a_sb = [a_sb0, a_sb1, a_sb2]

    PS0 = nc.alloc_psum_tensor("PS0", [P, 512], F32)
    PS0b = nc.alloc_psum_tensor("PS0b", [P, 512], F32)
    PS1 = nc.alloc_psum_tensor("PS1", [P, 512], F32)
    PS1b = nc.alloc_psum_tensor("PS1b", [P, 512], F32)
    PS2 = nc.alloc_psum_tensor("PS2", [P, 1024], BF16)
    PS2b = nc.alloc_psum_tensor("PS2b", [P, 1024], BF16)
    PS3 = nc.alloc_psum_tensor("PS3", [P, 512], F32)
    PS3b = nc.alloc_psum_tensor("PS3b", [P, 512], F32)
    PSQ = [PS0, PS0b]
    PSK = [PS1, PS1b]
    PSV = [PS3, PS3b]
    PST = [PS2, PS2b]

    B = Builder()

    def _close_early(aps=()):
        w = [(s, v) for s, v in B.count.items() if s in ("T", "V", "A", "G")]
        tok = B.op("G", lambda g: g.dma_start(out=o_hbm[0:P, :], in_=junk[:, :]),
                   waits=w, inc="dOut")
        for j, ap in enumerate(aps):
            tok = B.op("G", lambda g, ap=ap, j=j: g.dma_start(
                out=o_hbm[(j + 1) * P:(j + 2) * P, 0:ap.shape[-1]], in_=ap),
                inc="dOut")
        B.op("S", lambda s: s.nop(), waits=[tok])
        from contextlib import ExitStack
        sem_names = sorted(DMA_SEMS) + ["cc", "T", "V", "A", "G"]
        with ExitStack() as stack:
            sems = {n: stack.enter_context(nc.semaphore(f"sem_{n}")) for n in sem_names}
            blk = stack.enter_context(nc.Block())
            B.replay(blk, sems)
        return nc


    def dma(dst_ap, src_ap, sem="dWc", waits=()):
        return B.op("G", lambda g, d=dst_ap, s=src_ap: g.dma_start(out=d, in_=s),
                    waits=waits, inc=sem)

    dma(ident[:, :], ident_h, sem="dWe")
    dma(onescol[:, :], ones_h, sem="dWe")
    t_cvals = dma(cvals[:, :], cvals_h, sem="dWe")
    B.op("A", lambda a: a.nop(), waits=[t_cvals])
    wq_tok = [dma(wreg[:, c * 3072:(c + 1) * 3072], wqkvT[c * P:(c + 1) * P, :],
                  sem=f"dWq{c}") for c in range(8)]
    dma(maskM[:, :], maskM_h)
    dma(maskM2[:, :], maskM2_h)
    dma(tri16[:, :], tri16_h)

    dma(smask[:, :], smask_h)
    dma(rmask[:, :], rmask_h)
    t_consts = dma(rb_bc[:, :], rb_h.partition_broadcast(P))

    def dma_x(tau, buf, waits=()):
        return B.op("S", lambda s, t=tau, b=buf: s.dma_start(
            out=b[:, :], in_=x_hbm[t * P:(t + 1) * P, :]), waits=waits,
            inc=f"dX{tau % 3}")

    ldtok = {}

    def ln_stats(cols, w_in=()):
        B.op("V", lambda v, cols=cols: v.tensor_scalar_mul(cols[:, 2:3], cols[:, 0:1], -1.0 / D),
             waits=w_in)
        B.op("V", lambda v, cols=cols: v.tensor_scalar_mul(cols[:, 3:4], cols[:, 1:2], 1.0 / D))
        B.op("V", lambda v, cols=cols: v.tensor_mul(cols[:, 4:5], cols[:, 2:3], cols[:, 2:3]))
        tv = B.op("V", lambda v, cols=cols: v.tensor_sub(cols[:, 5:6], cols[:, 3:4], cols[:, 4:5]),
                  inc="V")
        B.op("A", lambda a, cols=cols: a.activation(cols[:, 6:7], cols[:, 5:6], AF.Ln,
                                                    bias=EPS_LN), waits=[tv])
        ta = B.op("A", lambda a, cols=cols: a.activation(cols[:, 7:8], cols[:, 6:7], AF.Exp,
                                                         scale=-0.5), inc="A")
        return B.op("V", lambda v, cols=cols: v.tensor_mul(cols[:, 4:5], cols[:, 2:3],
                                                           cols[:, 7:8]), waits=[ta], inc="V")

    t_norm = {}
    t_tr = {}
    t_v_padd = {}
    t_stripe = {}
    t_stripea = {}
    t_strip_cons = {}
    t_qe_done = {}
    t_pq = {}
    t_pk = {}
    t_pv = {}

    def emit_ln1(tau):
        p3 = tau % 3
        p = tau % 2
        cols = colsv[p]
        xb = xin[p3]
        w0 = [ldtok[tau]]
        if tau >= 2:
            w0.append(t_tr[tau - 2])
        B.op("V", lambda v, xb=xb, cols=cols: v.reduce_sum(cols[:, 0:1], xb[:, :], axis=AX.X),
             waits=w0)
        B.op("V", lambda v, xb=xb, cols=cols: v.scalar_tensor_tensor(
            junk[:, :], xb[:, :], 1.0, xb[:, :], ALU.mult, ALU.mult,
            accum_out=cols[:, 1:2]))
        tb = ln_stats(cols)
        wn = [tb]
        if tau == 0:
            wn.append(t_consts)
        t_norm[tau] = B.op("A", lambda a, xb=xb, cols=cols, p=p: a.activation(
            xhat[p][:, :], xb[:, :], AF.Identity, bias=cols[:, 4:5], scale=cols[:, 7:8]),
            waits=wn, inc="A")

    def emit_softmax(tau):
        qeb = [qe, qe2][tau % 2]
        B.op("V", lambda v, qeb=qeb: v.reduce_sum(
            sseg[:, :], qeb[:, :].rearrange("p (h c) -> p h c", h=16), axis=AX.X),
            waits=[t_pq[tau]])
        B.op("V", lambda v: v.reciprocal(rqb[:, :], sseg[:, :]))
        for h in range(16):
            t_qe_done[tau] = B.op("V", lambda v, h=h, tau=tau, qeb=qeb: v.tensor_scalar(
                qsm[:, tau * D + h * 64: tau * D + (h + 1) * 64],
                qeb[:, h * 64:(h + 1) * 64], rqb[:, h:h + 1], 0.125, ALU.mult, ALU.mult),
                inc=("V" if h == 15 else None))

    t_srun0 = B.op("G", lambda g: g.memset(srun[:, :], 0.0), inc="G")
    for tau in range(3):
        ldtok[tau] = dma_x(tau, xin[tau])
    emit_ln1(0)
    emit_ln1(1)

    for tau in range(NT):
        p = tau % 2
        wt = [t_norm[tau]]
        if tau == 0:
            wt.append(t_cvals)
        if tau >= 1:
            wt.append(t_stripe[tau - 1])
            wt.append(t_stripea[tau - 1])
        tt = None
        for g in range(8):
            tt = B.op("T", lambda t, g=g, p=p: t.transpose(
                PST[g % 2][:, (g // 2) * P:(g // 2 + 1) * P],
                xhat[p][:, g * P:(g + 1) * P], ident[:, :]),
                waits=(wt if g == 0 else ()), inc=("T" if g == 7 else None))
        t_tr[tau] = tt
        ws = [t_tr[tau]]
        if tau >= 2:
            ws.append(t_strip_cons[tau - 2])
        sview = strip[p][:, :].rearrange("p (q r b) -> p r q b", r=2, b=P)
        t_stripea[tau] = B.op("A", lambda a, sview=sview: a.copy(
            sview[:, 1, :, :], PST[1][:, 0:512].rearrange("p (q b) -> p q b", b=P)),
            waits=ws, inc="A")
        t_stripe[tau] = B.op("V", lambda v, sview=sview: v.tensor_copy(
            sview[:, 0, :, :], PST[0][:, 0:512].rearrange("p (q b) -> p q b", b=P)),
            waits=ws, inc="V")
        if tau + 2 < NT:
            emit_ln1(tau + 2)
        if tau + 3 < NT:
            ldtok[tau + 3] = dma_x(tau + 3, xin[(tau + 3) % 3], waits=[t_norm[tau]])
        banks6 = [PSQ[0], PSQ[1], PSK[0], PSK[1], PSV[0], PSV[1]]
        wmm = [t_stripe[tau], t_stripea[tau]]
        if tau >= 1:
            wmm += [t_pq[tau - 1], t_pk[tau - 1], t_pv[tau - 1], t_v_padd[tau - 1]]
        for c in range(8):
            for j in range(6):
                off = (j // 2) * 1024 + (j % 2) * 512
                wq = list(wmm) if (c == 0 and j == 0) else []
                if tau == 0 and j == 0:
                    wq.append(wq_tok[c])
                B.op("T", lambda t, p=p, c=c, j=j, off=off: t.matmul(
                    banks6[j][:, :], strip[p][:, c * P:(c + 1) * P],
                    wreg[:, c * 3072 + off: c * 3072 + off + 512],
                    start=(c == 0), stop=(c == 7)),
                    waits=wq, inc=("T" if (c == 7 and j == 5) else None))
        tmm = ("T", B.count["T"])
        t_strip_cons[tau] = tmm
        qeb = [qe, qe2][tau % 2]
        wq2 = [tmm]
        if tau >= 2:
            wq2.append(t_qe_done[tau - 2])
        B.op("A", lambda a, qeb=qeb: a.activation(qeb[:, 0:512], PSQ[0][:, :], AF.Exp),
             waits=wq2)
        t_pq[tau] = B.op("A", lambda a, qeb=qeb: a.activation(
            qeb[:, 512:1024], PSQ[1][:, :], AF.Exp), inc="A")
        B.op("A", lambda a, tau=tau: a.activation(
            ktm[:, tau * D: tau * D + 512], PSK[0][:, :], AF.Exp))
        t_pk[tau] = B.op("A", lambda a, tau=tau: a.activation(
            ktm[:, tau * D + 512: (tau + 1) * D], PSK[1][:, :], AF.Exp), inc="A")
        B.op("A", lambda a, tau=tau: a.copy(
            vtm[:, tau * D: tau * D + 512], PSV[0][:, :]))
        t_pv[tau] = B.op("A", lambda a, tau=tau: a.copy(
            vtm[:, tau * D + 512: (tau + 1) * D], PSV[1][:, :]), inc="A")
        wpa = [t_pq[tau], t_pk[tau], t_pv[tau]]
        if tau == 0:
            wpa.append(t_srun0)
        if tau >= 1:
            wpa.append(t_v_padd[tau - 1])
        for g in range(NG):
            B.op("T", lambda t, tau=tau, g=g: t.matmul(
                PSQ[0][0:64, g * 64:(g + 1) * 64],
                ktm[:, tau * D + g * P: tau * D + g * P + 64],
                vtm[:, tau * D + g * P: tau * D + g * P + 64],
                start=True, stop=True), waits=(wpa if g == 0 else ()))
            B.op("T", lambda t, tau=tau, g=g: t.matmul(
                PSQ[0][64:P, g * 64:(g + 1) * 64],
                ktm[:, tau * D + g * P + 64: tau * D + (g + 1) * P],
                vtm[:, tau * D + g * P + 64: tau * D + (g + 1) * P],
                start=True, stop=True))
            B.op("T", lambda t, tau=tau, g=g: t.matmul(
                PSQ[1][:, g:g + 1],
                ktm[:, tau * D + g * P: tau * D + (g + 1) * P],
                onescol[:, :], start=True, stop=True),
                inc=("T" if g == NG - 1 else None))
        t_pamm = ("T", B.count["T"])
        wsa = [t_v_padd[tau - 1]] if tau >= 1 else [t_srun0]
        for g in range(NG):
            tpa = B.op("A", lambda a, tau=tau, g=g: a.copy(
                spref_tmp[:, g * 1024 + tau * 64: g * 1024 + (tau + 1) * 64],
                srun[:, g * 64:(g + 1) * 64]),
                waits=(wsa if g == 0 else ()), inc=("A" if g == NG - 1 else None))
        for g in range(NG):
            B.op("V", lambda v, g=g: v.tensor_add(
                srun[:, g * 64:(g + 1) * 64], srun[:, g * 64:(g + 1) * 64],
                PSQ[0][:, g * 64:(g + 1) * 64]),
                waits=([t_pamm, tpa] if g == 0 else ()))
        t_v_padd[tau] = B.op("V", lambda v, tau=tau: v.tensor_copy(
            csums[:, tau * 8:(tau + 1) * 8], PSQ[1][:, 0:8]), inc="V")
        if tau >= 1:
            emit_softmax(tau - 1)
    emit_softmax(NT - 1)

    t_p12_T = ("T", B.count["T"])
    t_p12_A = ("A", B.count["A"])
    if limit == "p12":
        return _close_early([qsm[:, 0:1024], ktm[:, 0:1024], vtm[:, 0:1024],
                             qsm[:, 15 * 1024:16 * 1024], ktm[:, 15 * 1024:16 * 1024]])

    wout_tok = [dma(wout[:, c * 1024:(c + 1) * 1024], woutT[c * P:(c + 1) * P, :],
                    sem="dWo", waits=([t_p12_T] if c == 0 else ()))
                for c in range(8)]
    wout_tok = [wout_tok[-1]]

    prev_cz = None
    for g in range(NG):
        wv = [t_v_padd[NT - 1], t_stripe[NT - 1], t_stripea[NT - 1],
              t_pk[NT - 1]]
        if prev_cz is not None:
            wv.append(prev_cz)
        tcz = B.op("V", lambda v, g=g: v.tensor_copy(
            cz_sb[:, :],
            csums[:, :].rearrange("p (i g) -> p g i", g=NG)[:, g, :]),
            waits=wv, inc="V")
        czbank = PST[g % 2]
        ttr = B.op("T", lambda t, czbank=czbank: t.transpose(
            czbank[0:16, 0:P], cz_sb[:, :], ident[:, :]), waits=[tcz], inc="T")
        tev = B.op("V", lambda v, czbank=czbank: v.tensor_copy(
            czT_sb[:, :], czbank[0:16, 0:P]), waits=[ttr], inc="V")
        tmm = B.op("T", lambda t: t.matmul(PS1b[:, 0:17], czT_sb[:, :], tri16[:, :],
                                           start=True, stop=True), waits=[tev], inc="T")
        prev_cz = B.op("V", lambda v, g=g: v.tensor_scalar_add(
            zpref[:, g * 17:(g + 1) * 17], PS1b[:, 0:17], EPS_Z), waits=[tmm], inc="V")

    B.op("V", lambda v: v.tensor_scalar_mul(blob_s[:, 0:512], srun[:, :], smask[:, 0:1]),
         waits=[prev_cz, t_consts])
    tb2 = B.op("V", lambda v: v.tensor_scalar_mul(
        blob_s[:, 512:520],
        zpref[:, :].rearrange("p (g s) -> p g s", s=17)[:, :, 16], smask[:, 0:1]),
        inc="V")
    tsend = B.op("G", lambda g: g.dma_start(out=cc_in.ap(), in_=blob_s[:, :]),
                 waits=[tb2], inc="dWs")
    B.op("G", lambda g: g.collective_compute(
        "AllReduce", ALU.add, replica_groups=[[0, 1], [2, 3], [4, 5], [6, 7]],
        ins=[cc_in.ap().opt()], outs=[cc_out.ap().opt()]), waits=[tsend], inc="cc")
    trecv = B.op("G", lambda g: g.dma_start(out=brecv[:, :], in_=cc_out.ap()),
                 waits=[("cc", 1), t_p12_T], inc="dWr")

    t_pa_T = ("T", B.count["T"])
    t_pa_V = ("V", B.count["V"])
    t_pa_A = ("A", B.count["A"])
    if limit == "c2":
        return _close_early([spref[:, 0:1024], zpref[:, :], srun[:, :], brecv[:, :]])

    relZ = [t_pa_A, t_pa_A]
    relO = [t_pa_V, t_pa_V]
    relA_ = t_pa_V
    relI = [t_pa_A, t_pa_A]
    rel_tr_v = t_pa_V
    rel_tr_a = t_pa_A
    rel_qt = [t_pa_T, t_pa_T]
    rel_kc = [t_pa_T, t_pa_T]
    rel_ab = [t_pa_T, t_pa_T]
    rel_rt = [("G", B.count.get("G", 0))] * 2
    rel_tmpO = [t_pa_V, t_pa_V]

    nunits = NT * 4
    if limit.startswith("pbu"):
        nunits = int(limit[3:])

    ta_r_arr = [None, None]
    tv1_a = [None, None]
    ta1_a = [None, None]
    tg2_a = [None, None]
    tt2_a = [None, None]

    def st_tr(u):
        i, gg = u // 4, u % 4
        gA, gB = 2 * gg, 2 * gg + 1
        c0 = (u % 2) * 256
        nonlocal_w = [rel_tr[0], rel_tr[1]]
        B.op("T", lambda t, i=i, gA=gA, c0=c0: t.transpose(
            PS2[:, c0:c0 + P], qsm[:, i * D + gA * P: i * D + (gA + 1) * P],
            ident[:, :]), waits=nonlocal_w)
        B.op("T", lambda t, i=i, gA=gA, c0=c0: t.transpose(
            PS2b[:, c0:c0 + P], ktm[:, i * D + gA * P: i * D + (gA + 1) * P],
            ident[:, :]))
        B.op("T", lambda t, i=i, gB=gB, c0=c0: t.transpose(
            PS2[:, c0 + P:c0 + 256], qsm[:, i * D + gB * P: i * D + (gB + 1) * P],
            ident[:, :]))
        tt2_a[u % 2] = B.op("T", lambda t, i=i, gB=gB, c0=c0: t.transpose(
            PS2b[:, c0 + P:c0 + 256], ktm[:, i * D + gB * P: i * D + (gB + 1) * P],
            ident[:, :]), inc="T")

    def st_evac(u):
        pp = u % 2
        c0 = (u % 2) * 256
        wv1 = [tt2_a[pp], rel_qt[pp]]
        wa1 = [tt2_a[pp], rel_kc[pp]]
        if u == 0:
            wv1.append(t_consts)
            wa1.append(t_consts)
        tv1_a[pp] = B.op("V", lambda v, pp=pp, c0=c0: v.tensor_copy(
            qt[pp][:, :], PS2[:, c0:c0 + 256]), waits=wv1, inc="V")
        ta1_a[pp] = B.op("A", lambda a, pp=pp, c0=c0: a.copy(
            kc[pp][:, :], PS2b[:, c0:c0 + 256]), waits=wa1, inc="A")
        rel_tr[0] = tv1_a[pp]
        rel_tr[1] = ta1_a[pp]

    def st_zr(u, first=False):
        i, gg = u // 4, u % 4
        gA, gB = 2 * gg, 2 * gg + 1
        pz = u % 2
        PSzu = [PS0, PS0b][pz]
        B.op("T", lambda t, PSzu=PSzu, i=i, gA=gA: t.matmul(
            PSzu[:, 0:P], ktm[:, i * D + gA * P: i * D + (gA + 1) * P],
            maskM[:, :], start=True, stop=True), waits=[relZ[pz], relOz[pz]])
        tz = B.op("T", lambda t, PSzu=PSzu, i=i, gB=gB: t.matmul(
            PSzu[:, P:256], ktm[:, i * D + gB * P: i * D + (gB + 1) * P],
            maskM[:, :], start=True, stop=True), inc="T")
        wr = [tz, rel_rt[pz]]
        if first:
            wr.append(t_merge)
        B.op("A", lambda a, pz=pz, PSzu=PSzu, g=gA, i=i: a.activation(
            rt[pz][:, 0:P], PSzu[:, 0:P], AF.Ln,
            bias=zpref[:, g * 17 + i: g * 17 + i + 1]), waits=wr)
        B.op("A", lambda a, pz=pz, PSzu=PSzu, g=gB, i=i: a.activation(
            rt[pz][:, P:256], PSzu[:, P:256], AF.Ln,
            bias=zpref[:, g * 17 + i: g * 17 + i + 1]))
        ta_r_arr[pz] = B.op("A", lambda a, pz=pz: a.activation(
            rt[pz][:, 0:256], rt[pz][:, 0:256], AF.Exp, scale=-1.0), inc="A")
        relZ[pz] = ta_r_arr[pz]

    def st_qmul(u):
        pp = u % 2
        tg2_a[pp] = B.op("G", lambda gp, pp=pp: gp.tensor_mul(
            qt[pp][:, :], qt[pp][:, :], rt[pp][:, :]),
            waits=[tv1_a[pp], ta_r_arr[pp], rel_rt[pp]], inc="G")
        rel_rt[pp] = tg2_a[pp]

    t_xo_last = None

    def st_core(u):
        nonlocal t_xo_last
        global_ = None
        i, gg = u // 4, u % 4
        gA, gB = 2 * gg, 2 * gg + 1
        pp = u % 2
        PSzi = [PS0, PS0b][(u + 1) % 2]
        PSi = [PS3, PS3b][pp]
        c0 = (u % 2) * 256
        B.op("T", lambda t, pp=pp, c0=c0: t.matmul(
            PS1[:, c0:c0 + P], kc[pp][0:64, 0:P], qt[pp][0:64, 0:P],
            start=True, stop=True), waits=[ta1_a[pp], tg2_a[pp], relA_[0]])
        B.op("T", lambda t, pp=pp, c0=c0: t.matmul(
            PS1b[:, c0:c0 + P], kc[pp][64:P, 0:P], qt[pp][64:P, 0:P],
            start=True, stop=True))
        B.op("T", lambda t, pp=pp, c0=c0: t.matmul(
            PS1[:, c0 + P:c0 + 256], kc[pp][0:64, P:256], qt[pp][0:64, P:256],
            start=True, stop=True))
        tA = B.op("T", lambda t, pp=pp, c0=c0: t.matmul(
            PS1b[:, c0 + P:c0 + 256], kc[pp][64:P, P:256], qt[pp][64:P, P:256],
            start=True, stop=True), inc="T")
        rel_kc[pp] = tA
        B.op("V", lambda v, pp=pp, c0=c0: v.tensor_mul(
            ab[pp][:, 0:256], PS1[:, c0:c0 + 256], maskM2[:, :]),
            waits=[tA, rel_ab[pp]])
        tv4 = B.op("V", lambda v, pp=pp, c0=c0: v.tensor_mul(
            ab[pp][:, 256:512], PS1b[:, c0:c0 + 256], maskM2[:, :]), inc="V")
        relA_[0] = tv4
        B.op("T", lambda t, pp=pp, PSi=PSi, i=i, gA=gA: t.matmul(
            PSi[0:64, 0:P], vtm[:, i * D + gA * P: i * D + gA * P + 64],
            ab[pp][:, 0:P], start=True, stop=True), waits=[tv4, relI[pp]])
        B.op("T", lambda t, pp=pp, PSi=PSi, i=i, gA=gA: t.matmul(
            PSi[64:P, 0:P], vtm[:, i * D + gA * P + 64: i * D + (gA + 1) * P],
            ab[pp][:, 256:384], start=True, stop=True))
        B.op("T", lambda t, pp=pp, PSi=PSi, i=i, gB=gB: t.matmul(
            PSi[0:64, P:256], vtm[:, i * D + gB * P: i * D + gB * P + 64],
            ab[pp][:, P:256], start=True, stop=True))
        tI = B.op("T", lambda t, pp=pp, PSi=PSi, i=i, gB=gB: t.matmul(
            PSi[64:P, P:256], vtm[:, i * D + gB * P + 64: i * D + (gB + 1) * P],
            ab[pp][:, 384:512], start=True, stop=True), inc="T")
        rel_ab[pp] = tI
        tvo = B.op("V", lambda v, pp=pp, PSi=PSi: v.tensor_copy(
            tmpO[pp][:, :], PSi[:, 0:256]), waits=[tI, rel_tmpO[pp]], inc="V")
        relI[pp] = tvo
        B.op("T", lambda t, PSzi=PSzi, pp=pp, i=i, gA=gA: t.matmul(
            PSzi[0:64, 256:384],
            spref[0:64, gA * 1024 + i * 64: gA * 1024 + (i + 1) * 64],
            qt[pp][0:64, 0:P], start=True, stop=True),
            waits=[tg2_a[pp], relZ[(u + 1) % 2], relOz[(u + 1) % 2]])
        B.op("T", lambda t, PSzi=PSzi, pp=pp, i=i, gA=gA: t.matmul(
            PSzi[64:P, 256:384],
            spref[64:P, gA * 1024 + i * 64: gA * 1024 + (i + 1) * 64],
            qt[pp][64:P, 0:P], start=True, stop=True))
        B.op("T", lambda t, PSzi=PSzi, pp=pp, i=i, gB=gB: t.matmul(
            PSzi[0:64, 384:512],
            spref[0:64, gB * 1024 + i * 64: gB * 1024 + (i + 1) * 64],
            qt[pp][0:64, P:256], start=True, stop=True))
        tJ = B.op("T", lambda t, PSzi=PSzi, pp=pp, i=i, gB=gB: t.matmul(
            PSzi[64:P, 384:512],
            spref[64:P, gB * 1024 + i * 64: gB * 1024 + (i + 1) * 64],
            qt[pp][64:P, P:256], start=True, stop=True), inc="T")
        rel_qt[pp] = tJ
        xodst = xo[:, :].rearrange("p (g t) -> p g t", t=2048)[
            :, gA:gB + 1, i * P:(i + 1) * P]
        t_xo_last = tv5 = B.op("V", lambda v, PSzi=PSzi, pp=pp, xodst=xodst: v.tensor_add(
            xodst, tmpO[pp][:, :].rearrange("p (g t) -> p g t", t=P),
            PSzi[:, 256:512].rearrange("p (g t) -> p g t", t=P)),
            waits=[tJ], inc="V")
        relOz[(u + 1) % 2] = tv5

    relZ = [t_pa_A, t_pa_A]
    relOz = [t_pa_V, t_pa_V]
    relA_ = [t_pa_V]
    relI = [t_pa_A, t_pa_A]
    rel_tr = [t_pa_V, t_pa_A]
    rel_qt = [t_pa_T, t_pa_T]
    rel_kc = [t_pa_T, t_pa_T]
    rel_ab = [t_pa_T, t_pa_T]
    rel_rt = [("G", B.count.get("G", 0))] * 2
    rel_tmpO = [t_pa_V, t_pa_V]

    st_tr(0)
    st_evac(0)
    if nunits > 1:
        st_tr(1)
        st_evac(1)
    B.op("V", lambda v: v.tensor_scalar_mul(rqb[:, 0:8], brecv[:, 512:520],
                                            rmask[:, 0:1]), waits=[trecv])
    for g in range(NG):
        B.op("V", lambda v, g=g: v.tensor_scalar_add(
            zpref[:, g * 17:(g + 1) * 17], zpref[:, g * 17:(g + 1) * 17],
            rqb[:, g:g + 1]))
    for i in range(NT):
        tok = B.op("V", lambda v, i=i: v.scalar_tensor_tensor(
            spref[:, :].rearrange("p (g i d) -> p g i d", i=NT, d=64)[:, :, i, :],
            brecv[:, 0:512].rearrange("p (g d) -> p g d", d=64), rmask[:, 0:1],
            spref_tmp[:, :].rearrange("p (g i d) -> p g i d", i=NT, d=64)[:, :, i, :],
            ALU.mult, ALU.add))
    t_merge = tok

    st_zr(0, first=True)
    if nunits > 1:
        st_zr(1)
    st_qmul(0)
    for u in range(nunits):
        if 1 < u + 1 < nunits:
            st_tr(u + 1)
            st_evac(u + 1)
        if u + 2 < nunits:
            st_zr(u + 2)
        if u + 1 < nunits:
            st_qmul(u + 1)
        st_core(u)

    t_pb_T = ("T", B.count["T"])
    t_pb_V = ("V", B.count["V"])
    t_pb_A = ("A", B.count["A"])
    t_pb_G = ("G", B.count["G"])
    if limit == "pb" or limit.startswith("pbu") or limit.startswith("pbo"):
        return _close_early([xo[:, 0:1024], xo[:, 15 * 1024:16 * 1024]])

    dma(lorab_sb[:, :], lraB, sem="dWf", waits=[t_pb_G])
    dma(rw_sb[:, :].rearrange("p (c o) -> p c o", c=8),
        rw_h.rearrange("(c p) o -> p c o", p=P), sem="dWf")
    t_wffn_small = dma(bb_bc[:, :], bb_h.partition_broadcast(P), sem="dWf")
    wbase_tok = [dma(wbase[:, c * 1024:(c + 1) * 1024], baseT[c * P:(c + 1) * P, :],
                     sem="dWb2")
                 for c in range(8)]
    wbase_tok = [wbase_tok[-1]]

    def x2slice(tau):
        if tau < 8:
            return x2a[:, tau * D:(tau + 1) * D]
        return x2b[:, (tau - 8) * D:(tau - 7) * D]

    ld4 = {}
    ld4[0] = dma_x(0, xin[0], waits=[t_pb_V, t_pb_T, t_pb_G, t_pb_A])
    ld4[1] = dma_x(1, xin[1])
    relPQ4 = [t_pb_V, t_pb_V]
    t_v_x2 = {}
    tae4 = {}

    def p4_epilogue(tau):
        par = tau % 3
        cols = cols4v[par]
        B.op("V", lambda v, cols=cols: v.tensor_add(cols[:, 0:1], cols[:, 0:1],
                                                    cols[:, 6:7]), waits=[tae4[tau]])
        B.op("V", lambda v, par=par, cols=cols: v.scalar_tensor_tensor(
            junk[:, :], a_sb[par][:, :], 1.0, a_sb[par][:, :], ALU.mult, ALU.mult,
            accum_out=cols[:, 1:2]))
        tb = ln_stats(cols)
        ta2 = B.op("A", lambda a, par=par, cols=cols: a.activation(
            t1[:, :], a_sb[par][:, :], AF.Identity, bias=cols[:, 4:5],
            scale=cols[:, 7:8]), waits=[tb], inc="A")
        wv = [ta2, ld4[tau]]
        if tau == 0:
            wv.append(t_pb_T)
        t_v_x2[tau] = B.op("V", lambda v, tau=tau: v.tensor_add(
            x2slice(tau), t1[:, :], xin[tau % 2][:, :]), waits=wv, inc="V")
        if tau + 2 < NT:
            ld4[tau + 2] = dma_x(tau + 2, xin[tau % 2], waits=[t_v_x2[tau]])

    for tau in range(NT):
        par = tau % 3
        PSo = [[PS0, PS0b], [PS1, PS1b]][tau % 2]
        cols = cols4v[par]
        wt = [relPQ4[tau % 2]]
        if tau == 0:
            wt += [t_xo_last] + wout_tok
        for c in range(8):
            for g2 in range(2):
                B.op("T", lambda t, g2=g2, c=c, tau=tau, PSo=PSo: t.matmul(
                    PSo[g2][:, :], xo[:, c * 2048 + tau * P: c * 2048 + (tau + 1) * P],
                    wout[:, c * 1024 + g2 * 512: c * 1024 + (g2 + 1) * 512],
                    start=(c == 0), stop=(c == 7)),
                    waits=(wt if (g2 == 0 and c == 0) else ()),
                    inc=("T" if (g2 == 1 and c == 7) else None))
        tmm = ("T", B.count["T"])
        wa = [tmm]
        if tau >= 3:
            wa.append(t_v_x2[tau - 3])
        B.op("A", lambda a, par=par, PSo=PSo, cols=cols: a.activation(
            a_sb[par][:, 0:512], PSo[0][:, :], AF.Copy, accum_out=cols[:, 0:1]),
            waits=wa)
        tae4[tau] = ta = B.op("A", lambda a, par=par, PSo=PSo, cols=cols: a.activation(
            a_sb[par][:, 512:1024], PSo[1][:, :], AF.Copy, accum_out=cols[:, 6:7]),
            inc="A")
        relPQ4[tau % 2] = ta
        if tau >= 1:
            p4_epilogue(tau - 1)
    p4_epilogue(NT - 1)

    t_p4_V = ("V", B.count["V"])
    if limit == "p4":
        return _close_early([x2a[:, 0:1024], x2b[:, 7 * 1024:8 * 1024]])

    relPF = [t_p4_V, t_p4_V]
    relPR = [t_p4_V, t_p4_V]
    t_norm3 = {}
    t_tr3 = {}
    t_stripe3 = {}
    t_stripe3a = {}
    t_ctev = {}
    t_cb = {}
    t_ct_cons = {}
    t_strip_cons3 = {}
    t_x3 = {}
    tff = {}

    t_ln3_tb = {}

    def emit_ln3(tau):
        par = tau % 2
        cols = cols3v[par]
        xsl = x2slice(tau)
        wv = [t_v_x2[tau]]
        B.op("V", lambda v, xsl=xsl, cols=cols: v.reduce_sum(cols[:, 0:1], xsl, axis=AX.X),
             waits=wv)
        wg = [t_v_x2[tau]]
        if tau >= 2:
            wg.append(t_ln3_tb[tau - 2])
        if tau < 2:
            wg.append(t_pb_T)
        tg = B.op("A", lambda a, xsl=xsl, cols=cols: a.activation(
            junk2[:, :], xsl, AF.Square, accum_out=cols[:, 1:2]),
            waits=wg, inc="A")
        tb = ln_stats(cols, w_in=[tg])
        t_ln3_tb[tau] = tb
        wn = [tb]
        if tau >= 2:
            wn.append(t_tr3[tau - 2])
        t_norm3[tau] = B.op("A", lambda a, xsl=xsl, cols=cols, par=par: a.activation(
            xhat[par][:, :], xsl, AF.Identity, bias=cols[:, 4:5], scale=cols[:, 7:8]),
            waits=wn, inc="A")

    def p56_epilogue(tau):
        par = tau % 3
        cols = cols4v[par]
        PSf = [[PS0, PS0b], [PS1, PS1b]][tau % 2]
        xsl = x2slice(tau)
        wv2 = [tff[tau]]
        if tau >= 3:
            wv2.append(t_x3[tau - 3])
        if tau == 0:
            wv2.append(t_wffn_small)
        B.op("V", lambda v, par=par, PSf=PSf, cols=cols: v.scalar_tensor_tensor(
            a_sb[par][:, 0:512], PSf[0][:, :], 1.0, bb_bc[:, 0:512], ALU.mult, ALU.add,
            accum_out=cols[:, 0:1]), waits=wv2)
        tv3 = B.op("V", lambda v, par=par, PSf=PSf, cols=cols: v.scalar_tensor_tensor(
            a_sb[par][:, 512:1024], PSf[1][:, :], 1.0, bb_bc[:, 512:1024], ALU.mult,
            ALU.add, accum_out=cols[:, 3:4]), inc="V")
        relPF[tau % 2] = tv3
        B.op("V", lambda v, cols=cols: v.tensor_add(cols[:, 0:1], cols[:, 0:1],
                                                    cols[:, 3:4]))
        tsq = B.op("A", lambda a, par=par, cols=cols: a.activation(
            junk[:, :], a_sb[par][:, :], AF.Square, accum_out=cols[:, 1:2]),
            waits=[tv3], inc="A")
        tb2 = ln_stats(cols, w_in=[tsq])
        ta2 = B.op("A", lambda a, par=par, cols=cols: a.activation(
            t1[:, :], a_sb[par][:, :], AF.Identity, bias=cols[:, 4:5],
            scale=cols[:, 7:8]), waits=[tb2], inc="A")
        t_x3[tau] = B.op("V", lambda v, xsl=xsl: v.tensor_add(xsl, xsl, t1[:, :]),
                         waits=[ta2], inc="V")
        B.op("S", lambda s, tau=tau, xsl=xsl: s.dma_start(
            out=o_hbm[tau * P:(tau + 1) * P, :], in_=xsl), waits=[t_x3[tau]],
            inc="dOut")

    emit_ln3(0)
    emit_ln3(1)
    for tau in range(NT):
        par = tau % 2
        cols = colsv[par]
        PSf = [[PS0, PS0b], [PS1, PS1b]][par]
        PSr = [PS3, PS3b][par]
        PSt = [PS2, PS2b][par]
        wt = [t_norm3[tau]]
        if tau >= 1:
            wt.append(t_stripe3[tau - 1])
            wt.append(t_stripe3a[tau - 1])
            wt.append(t_ctev[tau - 1])
        else:
            wt += [t_pb_V, t_pb_A]
        tt = None
        for g in range(8):
            tt = B.op("T", lambda t, g=g, par=par, tau=tau: t.transpose(
                PST[(g + tau) % 2][:, (g // 2) * P:(g // 2 + 1) * P],
                xhat[par][:, g * P:(g + 1) * P], ident[:, :]),
                waits=(wt if g == 0 else ()), inc=("T" if g == 7 else None))
        t_tr3[tau] = tt
        ws = [t_tr3[tau]]
        if tau >= 2:
            ws.append(t_strip_cons3[tau - 2])
        else:
            ws.append(t_pb_G)
        sview = strip[par][:, :].rearrange("p (q r b) -> p r q b", r=2, b=P)
        t_stripe3a[tau] = B.op("A", lambda a, sview=sview, tau=tau: a.copy(
            sview[:, 1 - tau % 2, :, :],
            PST[1][:, 0:512].rearrange("p (q b) -> p q b", b=P)), waits=ws, inc="A")
        t_stripe3[tau] = B.op("A", lambda a, sview=sview, tau=tau: a.copy(
            sview[:, tau % 2, :, :], PST[0][:, 0:512].rearrange("p (q b) -> p q b", b=P)),
            waits=ws, inc="A")
        wr = [t_stripe3[tau], t_stripe3a[tau],
              (t_cb[tau - 2] if tau >= 2 else relPR[par])]
        if tau == 0:
            wr.append(t_wffn_small)
        for c in range(8):
            B.op("T", lambda t, c=c, par=par, PSr=PSr: t.matmul(
                PSr[:, 0:16], strip[par][:, c * P:(c + 1) * P],
                rw_sb[:, c * 16:(c + 1) * 16], start=(c == 0), stop=(c == 7)),
                waits=(wr if c == 0 else ()), inc=("T" if c == 7 else None))
        tr = ("T", B.count["T"])
        wb = [relPF[par]]
        if tau == 0:
            wb += wbase_tok
        for c in range(8):
            for g2 in range(2):
                t_base = B.op("T", lambda t, g2=g2, c=c, par=par, PSf=PSf: t.matmul(
                    PSf[g2][:, :], strip[par][:, c * P:(c + 1) * P],
                    wbase[:, c * 1024 + g2 * 512: c * 1024 + (g2 + 1) * 512],
                    start=(c == 0), stop=False),
                    waits=(wb if (g2 == 0 and c == 0) else ()),
                    inc=("T" if (c == 7 and g2 == 1) else None))
        t_strip_cons3[tau] = t_base
        wlg = [tr]
        if tau == 0:
            wlg.append(t_consts)
        tv = B.op("V", lambda v, PSr=PSr: v.tensor_add(lgb[:, :], PSr[:, 0:8], rb_bc[:, :]),
                  waits=wlg, inc="V")
        ta = B.op("A", lambda a: a.activation(geb[:, :], lgb[:, :], AF.Exp,
                                              accum_out=gsb[:, :]), waits=[tv], inc="A")
        B.op("V", lambda v: v.reciprocal(gqb[:, :], gsb[:, :]), waits=[ta])
        tv2 = B.op("V", lambda v: v.tensor_scalar_mul(gbuf[:, :], geb[:, :], gqb[:, 0:1]))
        tv2 = B.op("V", lambda v: v.tensor_copy(gbuf32[:, :], gbuf[:, :]), inc="V")
        wc = [tv2]
        if tau >= 2:
            wc.append(t_ct_cons[tau - 2])
        tcb = None
        for e in range(8):
            tcb = B.op("V", lambda v, e=e, par=par, PSr=PSr: v.tensor_scalar(
                cbuf[par][:, e * 8:(e + 1) * 8], PSr[:, 8:16], gbuf32[:, e:e + 1], 2.0,
                ALU.mult, ALU.mult), waits=(wc if e == 0 else ()),
                inc=("V" if e == 7 else None))
        t_cb[tau] = tcb
        cslot = PST[(tau + 1) % 2][0:64, 512:512 + P]
        wct = [tcb]
        if tau >= 2:
            wct.append(t_ctev[tau - 2])
        tct = B.op("T", lambda t, cslot=cslot, par=par: t.transpose(
            cslot, cbuf[par][:, :], ident[:, :]), waits=wct, inc="T")
        t_ctev[tau] = tce = B.op("V", lambda v, cslot=cslot, par=par: v.tensor_copy(
            ctsb[par][:, :], cslot), waits=[tct], inc="V")
        for g2 in range(2):
            tff[tau] = B.op("T", lambda t, g2=g2, par=par, PSf=PSf: t.matmul(
                PSf[g2][:, :], ctsb[par][:, :],
                lorab_sb[:, g2 * 512:(g2 + 1) * 512], start=False, stop=True),
                waits=([tce] if g2 == 0 else ()),
                inc=("T" if g2 == 1 else None))
        t_ct_cons[tau] = tff[tau]
        relPR[par] = t_cb[tau]
        if tau + 2 < NT:
            emit_ln3(tau + 2)
        if tau >= 1:
            p56_epilogue(tau - 1)
    p56_epilogue(NT - 1)

    B.op("S", lambda s: s.nop(), waits=[("dOut", 16 * NT)])

    from contextlib import ExitStack
    sem_names = sorted(DMA_SEMS) + ["cc", "T", "V", "A", "G"]
    with ExitStack() as stack:
        sems = {n: stack.enter_context(nc.semaphore(f"sem_{n}")) for n in sem_names}
        blk = stack.enter_context(nc.Block())
        B.replay(blk, sems)
    return nc


_NC_CACHE = None


def kernel(L0_attn_x, L0_attn_W_qkv, L0_attn_W_out, L0_ffn_base_w, L0_ffn_base_b,
           L0_ffn_lora_A, L0_ffn_lora_B, L0_ffn_router_w, L0_ffn_router_b):
    global _NC_CACHE
    x = np.ascontiguousarray(np.asarray(L0_attn_x, dtype=np.float32))
    wqkvT = np.ascontiguousarray(np.asarray(L0_attn_W_qkv, dtype=np.float32).T)
    woutT = np.ascontiguousarray(np.asarray(L0_attn_W_out, dtype=np.float32).T)
    baseT = np.ascontiguousarray(np.asarray(L0_ffn_base_w, dtype=np.float32).T)
    rw = np.ascontiguousarray(np.concatenate(
        [np.asarray(L0_ffn_router_w, dtype=np.float32).T,
         np.asarray(L0_ffn_lora_A, dtype=np.float32)], axis=1))
    lorab = np.ascontiguousarray(np.asarray(L0_ffn_lora_B, dtype=np.float32).reshape(64, 1024))
    rb = np.ascontiguousarray(np.asarray(L0_ffn_router_b, dtype=np.float32).reshape(1, 8))
    bb = np.ascontiguousarray(np.asarray(L0_ffn_base_b, dtype=np.float32).reshape(1, 1024))

    ident = np.eye(P, dtype=np.float32)
    s_idx = np.arange(P)[:, None]
    t_idx = np.arange(P)[None, :]
    maskM = (s_idx <= t_idx).astype(np.float32)
    maskM2 = np.concatenate([maskM, maskM], axis=1)
    tri16 = np.zeros((16, 17), dtype=np.float32)
    for j in range(16):
        tri16[j, j + 1:16] = 1.0
        tri16[j, 16] = 1.0
    ones_col = np.ones((P, 1), dtype=np.float32)
    EPS_LN_NP = 1e-5

    if _NC_CACHE is None:
        _NC_CACHE = build_nc()
    nc = _NC_CACHE

    in_maps = []
    for c in range(8):
        b, h = c // 2, c % 2
        send = 1.0 if h == 0 else 0.0
        in_maps.append({
            "x": np.ascontiguousarray(x[b, h * 2048:(h + 1) * 2048, :]),
            "wqkvT": wqkvT, "woutT": woutT, "baseT": baseT, "rw": rw,
            "loraB": lorab, "rb": rb, "bb": bb,
            "ident": ident, "maskM": maskM, "maskM2": maskM2, "tri16": tri16,
            "onescol": ones_col,
            "cvals": np.broadcast_to(np.array([EPS_LN_NP, -0.5], dtype=np.float32), (P, 2)).copy(),
            "smask": np.full((P, 1), send, dtype=np.float32),
            "rmask": np.full((P, 1), 1.0 - send, dtype=np.float32),
        })
    res = run_bass_kernel_spmd(nc, in_maps, core_ids=list(range(8)))
    out = np.empty((4, 4096, 1024), dtype=np.float32)
    for c in range(8):
        b, h = c // 2, c % 2
        out[b, h * 2048:(h + 1) * 2048, :] = res.results[c]["out"]
    return out


# version 17
# speedup vs baseline: 1.2748x; 1.2748x over previous
import numpy as np
import concourse.bass as bass
import concourse.mybir as mybir
from concourse.bass_utils import run_bass_kernel_spmd

F32 = mybir.dt.float32
BF16 = mybir.dt.bfloat16
AF = mybir.ActivationFunctionType
ALU = mybir.AluOpType
AX = mybir.AxisListType

P = 128
NT = 16
D = 1024
NG = 8
EPS_LN = 1e-5
EPS_Z = 1e-9

DMA_SEMS = {"dX0", "dX1", "dX2", "dWe", "dOut", "dWc", "dWo", "dWb2", "dWf", "dWs", "dWr",
            "dWq0", "dWq1", "dWq2", "dWq3", "dWq4", "dWq5", "dWq6", "dWq7"}


class Builder:

    def __init__(self):
        self.sched = []
        self.count = {}

    def op(self, eng, fn, waits=(), inc=None):
        if eng in ("V", "A"):
            assert inc in (None, eng)
            inc = eng
        w = []
        for t in waits:
            if t is None:
                continue
            s, v = t
            assert v <= self.count.get(s, 0), f"wait on future: {s} {v}>{self.count.get(s,0)}"
            w.append((s, v))
        tok = None
        if inc is not None:
            amt = 16 if inc in DMA_SEMS else 1
            self.count[inc] = self.count.get(inc, 0) + amt
            tok = (inc, self.count[inc])
        self.sched.append((eng, w, fn, inc))
        return tok

    def replay(self, blk, sems):
        for eng_key, eng_name in [("S", "sync"), ("G", "gpsimd"), ("V", "vector"),
                                  ("A", "scalar"), ("T", "tensor")]:
            ops = [o for o in self.sched if o[0] == eng_key]
            serial = eng_key in ("V", "A", "G")

            def body(eng, ops=ops, eng_key=eng_key, serial=serial):
                last = {}
                self_idx = 0
                for (_e, waits, fn, inc) in ops:
                    if serial and self_idx > 0:
                        eng.wait_ge(sems[eng_key], self_idx)
                    for (s, v) in waits:
                        if s == eng_key:
                            continue
                        if last.get(s, -1) >= v:
                            continue
                        eng.wait_ge(sems[s], v)
                        last[s] = v
                    inst = fn(eng)
                    if inc is not None:
                        if inc == "cc":
                            inst.then_inc(sems[inc])
                        else:
                            inst.then_inc(sems[inc], 16 if inc in DMA_SEMS else 1)
                    if inc == eng_key:
                        self_idx += 1
            getattr(blk, eng_name)(body)


def build_nc(limit="all"):
    nc = bass.Bass("TRN2", target_bir_lowering=False, debug=False)

    x_hbm = nc.dram_tensor("x", [2048, D], F32, kind="ExternalInput").ap()
    wqkvT = nc.dram_tensor("wqkvT", [D, 3072], F32, kind="ExternalInput").ap()
    woutT = nc.dram_tensor("woutT", [D, D], F32, kind="ExternalInput").ap()
    baseT = nc.dram_tensor("baseT", [D, D], F32, kind="ExternalInput").ap()
    rw_h = nc.dram_tensor("rw", [D, 16], F32, kind="ExternalInput").ap()
    lraB = nc.dram_tensor("loraB", [64, D], F32, kind="ExternalInput").ap()
    rb_h = nc.dram_tensor("rb", [1, 8], F32, kind="ExternalInput").ap()
    bb_h = nc.dram_tensor("bb", [1, D], F32, kind="ExternalInput").ap()
    ident_h = nc.dram_tensor("ident", [P, P], F32, kind="ExternalInput").ap()
    maskM_h = nc.dram_tensor("maskM", [P, P], F32, kind="ExternalInput").ap()
    maskM2_h = nc.dram_tensor("maskM2", [P, 256], F32, kind="ExternalInput").ap()
    tri16_h = nc.dram_tensor("tri16", [16, 17], F32, kind="ExternalInput").ap()
    ones_h = nc.dram_tensor("onescol", [P, 1], F32, kind="ExternalInput").ap()
    smask_h = nc.dram_tensor("smask", [P, 1], F32, kind="ExternalInput").ap()
    rmask_h = nc.dram_tensor("rmask", [P, 1], F32, kind="ExternalInput").ap()
    cvals_h = nc.dram_tensor("cvals", [P, 2], F32, kind="ExternalInput").ap()
    o_hbm = nc.dram_tensor("out", [2048, D], F32, kind="ExternalOutput").ap()
    cc_in = nc.dram_tensor("cc_in", [P, 520], F32)
    cc_out = nc.dram_tensor("cc_out", [P, 520], F32)

    def alias(name, shape, dtype, base_handle, byte_off=0):
        h = nc.virtual_sbuf_tensor(name, shape, dtype)
        nc._tensor(h.name, list(h.shape), h.dtype, type="SB")
        mloc = nc.lookup_mloc(h)
        mloc.allocated = True
        mloc.addr = nc.lookup_mloc(base_handle).addr + byte_off
        return h

    alloc = nc.alloc_sbuf_tensor
    ident = alloc("identsb", [P, P], BF16)
    maskM = alloc("maskMsb", [P, P], BF16)
    maskM2 = alloc("maskM2sb", [P, 256], F32)
    tri16 = alloc("tri16sb", [16, 17], BF16)
    onescol = alloc("onescolsb", [P, 1], BF16)
    smask = alloc("smasksb", [P, 1], F32)
    rmask = alloc("rmasksb", [P, 1], F32)
    rb_bc = alloc("rb_bc", [P, 8], F32)
    cvals = alloc("cvalssb", [P, 2], F32)
    nc.const_aps.register(F32, EPS_LN, cvals[:, 0:1])
    nc.const_aps.register(F32, -0.5, cvals[:, 1:2])
    wreg = alloc("wreg", [P, 24576], BF16)
    qsm = alloc("qsm", [P, NT * D], BF16)
    ktm = alloc("ktm", [P, NT * D], BF16)
    vtm = alloc("vtm", [P, NT * D], BF16)
    xo = alloc("xo", [P, NG * 2048], BF16)
    zpref = alloc("zpref", [P, NG * 17], F32)
    srun = alloc("srun", [P, 512], F32)
    xin0 = alloc("xin0", [P, D], F32)
    xin1 = alloc("xin1", [P, D], F32)
    xhat0 = alloc("xhat0", [P, D], BF16)
    xhat1 = alloc("xhat1", [P, D], BF16)
    strip0 = alloc("strip0", [P, D], BF16)
    strip1 = alloc("strip1", [P, D], BF16)
    junk = alloc("junk", [P, D], F32)
    qe = alloc("qe", [P, D], F32)
    cols0 = alloc("cols0", [P, 8], F32)
    cols1 = alloc("cols1", [P, 8], F32)
    sseg = alloc("sseg", [P, 16], F32)
    rqb = alloc("rqbuf", [P, 16], F32)
    cz_sb = alloc("cz_sb", [P, 16], BF16)
    csums = alloc("csums", [P, P], F32)
    czT_sb = alloc("czT_sb", [16, P], BF16)

    xhat = [xhat0, xhat1]
    strip = [strip0, strip1]
    colsv = [cols0, cols1]

    x2a = alias("x2a", [P, 8 * D], F32, qsm)
    x2b = alias("x2b", [P, 8 * D], F32, ktm)
    wout = alias("wout", [P, 8192], BF16, wreg, 0)
    wbase = alias("wbase", [P, 8192], BF16, wreg, 16384)
    spref = alias("spref", [P, 8192], BF16, wreg, 32768)
    lorab_sb = alias("lorabsb", [64, D], BF16, wreg, 32768)
    rw_sb = alias("rw_sb", [P, 128], BF16, wreg, 32768 + 2048)
    bb_bc = alias("bb_bc", [P, D], F32, wreg, 32768 + 2048 + 256)
    t1 = alias("t1", [P, D], F32, junk)
    qe2 = alias("qe2", [P, D], F32, xo, 0)
    xin2 = alias("xin2", [P, D], F32, xo, 4096)
    spref_tmp = alias("spref_tmp", [P, 8192], BF16, xo, 8192)

    xin = [xin0, xin1, xin2]
    blob_s = alias("blob_s", [P, 520], F32, xhat0)
    brecv = alias("brecv", [P, 520], F32, strip0)
    qt = [alias(f"qt{j}", [P, 256], BF16, xin0, j * 512) for j in range(2)]
    kc = [alias(f"kc{j}", [P, 256], BF16, xin0, 1024 + j * 512) for j in range(2)]
    ab = [alias(f"ab{j}", [P, 512], BF16, xin0, 2048 + j * 1024) for j in range(2)]
    rt = [alias(f"rt{j}", [P, 256], F32, xin1, j * 1024) for j in range(2)]
    tmpO = [alias(f"tmpO{j}", [P, 256], F32, xin1, 2048 + j * 1024) for j in range(2)]
    a_sb0 = alias("a_sb0", [P, D], F32, vtm, 0)
    a_sb1 = alias("a_sb1", [P, D], F32, vtm, 4096)
    a_sb2 = alias("a_sb2", [P, D], F32, vtm, 20480)
    cols4v = [alias(f"cols4{j}", [P, 8], F32, vtm, 24576 + j * 32) for j in range(3)]
    cbuf = [alias(f"cbuf{j}", [P, 64], BF16, vtm, 8192 + j * 128) for j in range(2)]
    ctsb = [alias(f"ctsb{j}", [64, P], BF16, vtm, 8448 + j * 256) for j in range(2)]
    lgb = alias("lgb", [P, 8], F32, vtm, 8960)
    geb = alias("geb", [P, 8], F32, vtm, 8992)
    gsb = alias("gsb", [P, 1], F32, vtm, 9024)
    gqb = alias("gqb", [P, 1], F32, vtm, 9056)
    gbuf = alias("gbuf", [P, 8], BF16, vtm, 9088)
    gbuf32 = alias("gbuf32", [P, 8], F32, vtm, 9120)
    junk2 = alias("junk2", [P, D], F32, vtm, 12288)
    cols3v = [alias(f"cols3{j}", [P, 8], F32, vtm, 16384 + j * 32) for j in range(2)]
    a_sb = [a_sb0, a_sb1, a_sb2]

    PS0 = nc.alloc_psum_tensor("PS0", [P, 512], F32)
    PS0b = nc.alloc_psum_tensor("PS0b", [P, 512], F32)
    PS1 = nc.alloc_psum_tensor("PS1", [P, 512], F32)
    PS1b = nc.alloc_psum_tensor("PS1b", [P, 512], F32)
    PS2 = nc.alloc_psum_tensor("PS2", [P, 1024], BF16)
    PS2b = nc.alloc_psum_tensor("PS2b", [P, 1024], BF16)
    PS3 = nc.alloc_psum_tensor("PS3", [P, 512], F32)
    PS3b = nc.alloc_psum_tensor("PS3b", [P, 512], F32)
    PSQ = [PS0, PS0b]
    PSK = [PS1, PS1b]
    PSV = [PS3, PS3b]
    PST = [PS2, PS2b]

    B = Builder()

    def _close_early(aps=()):
        w = [(s, v) for s, v in B.count.items() if s in ("T", "V", "A", "G")]
        tok = B.op("G", lambda g: g.dma_start(out=o_hbm[0:P, :], in_=junk[:, :]),
                   waits=w, inc="dOut")
        for j, ap in enumerate(aps):
            tok = B.op("G", lambda g, ap=ap, j=j: g.dma_start(
                out=o_hbm[(j + 1) * P:(j + 2) * P, 0:ap.shape[-1]], in_=ap),
                inc="dOut")
        B.op("S", lambda s: s.nop(), waits=[tok])
        from contextlib import ExitStack
        sem_names = sorted(DMA_SEMS) + ["cc", "T", "V", "A", "G"]
        with ExitStack() as stack:
            sems = {n: stack.enter_context(nc.semaphore(f"sem_{n}")) for n in sem_names}
            blk = stack.enter_context(nc.Block())
            B.replay(blk, sems)
        return nc


    def dma(dst_ap, src_ap, sem="dWc", waits=()):
        return B.op("G", lambda g, d=dst_ap, s=src_ap: g.dma_start(out=d, in_=s),
                    waits=waits, inc=sem)

    dma(ident[:, :], ident_h, sem="dWe")
    dma(onescol[:, :], ones_h, sem="dWe")
    t_cvals = dma(cvals[:, :], cvals_h, sem="dWe")
    B.op("A", lambda a: a.nop(), waits=[t_cvals])
    wq_tok = [dma(wreg[:, c * 3072:(c + 1) * 3072], wqkvT[c * P:(c + 1) * P, :],
                  sem=f"dWq{c}") for c in range(8)]
    dma(maskM[:, :], maskM_h)
    dma(maskM2[:, :], maskM2_h)
    dma(tri16[:, :], tri16_h)

    dma(smask[:, :], smask_h)
    dma(rmask[:, :], rmask_h)
    t_consts = dma(rb_bc[:, :], rb_h.partition_broadcast(P))

    def dma_x(tau, buf, waits=()):
        return B.op("S", lambda s, t=tau, b=buf: s.dma_start(
            out=b[:, :], in_=x_hbm[t * P:(t + 1) * P, :]), waits=waits,
            inc=f"dX{tau % 3}")

    ldtok = {}

    def ln_stats(cols, w_in=()):
        B.op("V", lambda v, cols=cols: v.tensor_scalar_mul(cols[:, 2:3], cols[:, 0:1], -1.0 / D),
             waits=w_in)
        B.op("V", lambda v, cols=cols: v.tensor_scalar_mul(cols[:, 3:4], cols[:, 1:2], 1.0 / D))
        B.op("V", lambda v, cols=cols: v.tensor_mul(cols[:, 4:5], cols[:, 2:3], cols[:, 2:3]))
        tv = B.op("V", lambda v, cols=cols: v.tensor_sub(cols[:, 5:6], cols[:, 3:4], cols[:, 4:5]),
                  inc="V")
        B.op("A", lambda a, cols=cols: a.activation(cols[:, 6:7], cols[:, 5:6], AF.Ln,
                                                    bias=EPS_LN), waits=[tv])
        ta = B.op("A", lambda a, cols=cols: a.activation(cols[:, 7:8], cols[:, 6:7], AF.Exp,
                                                         scale=-0.5), inc="A")
        return B.op("V", lambda v, cols=cols: v.tensor_mul(cols[:, 4:5], cols[:, 2:3],
                                                           cols[:, 7:8]), waits=[ta], inc="V")

    t_norm = {}
    t_tr = {}
    t_v_padd = {}
    t_stripe = {}
    t_stripea = {}
    t_strip_cons = {}
    t_qe_done = {}
    t_pq = {}
    t_pk = {}
    t_pv = {}

    def emit_ln1(tau):
        p3 = tau % 3
        p = tau % 2
        cols = colsv[p]
        xb = xin[p3]
        w0 = [ldtok[tau]]
        if tau >= 2:
            w0.append(t_tr[tau - 2])
        B.op("V", lambda v, xb=xb, cols=cols: v.reduce_sum(cols[:, 0:1], xb[:, :], axis=AX.X),
             waits=w0)
        B.op("V", lambda v, xb=xb, cols=cols: v.scalar_tensor_tensor(
            junk[:, :], xb[:, :], 1.0, xb[:, :], ALU.mult, ALU.mult,
            accum_out=cols[:, 1:2]))
        tb = ln_stats(cols)
        wn = [tb]
        if tau == 0:
            wn.append(t_consts)
        t_norm[tau] = B.op("A", lambda a, xb=xb, cols=cols, p=p: a.activation(
            xhat[p][:, :], xb[:, :], AF.Identity, bias=cols[:, 4:5], scale=cols[:, 7:8]),
            waits=wn, inc="A")

    def emit_softmax(tau):
        qeb = [qe, qe2][tau % 2]
        B.op("V", lambda v, qeb=qeb: v.reduce_sum(
            sseg[:, :], qeb[:, :].rearrange("p (h c) -> p h c", h=16), axis=AX.X),
            waits=[t_pq[tau]])
        B.op("V", lambda v: v.reciprocal(rqb[:, :], sseg[:, :]))
        for h in range(16):
            t_qe_done[tau] = B.op("V", lambda v, h=h, tau=tau, qeb=qeb: v.tensor_scalar(
                qsm[:, tau * D + h * 64: tau * D + (h + 1) * 64],
                qeb[:, h * 64:(h + 1) * 64], rqb[:, h:h + 1], 0.125, ALU.mult, ALU.mult),
                inc=("V" if h == 15 else None))

    def emit_pa(tau):
        wpa = [t_pk[tau], t_pv[tau]]
        if tau == 0:
            wpa += [t_srun0, t_cvals]
        if tau >= 1:
            wpa.append(t_v_padd[tau - 1])
        for g in range(NG):
            B.op("T", lambda t, tau=tau, g=g: t.matmul(
                PSK[0][0:64, g * 64:(g + 1) * 64],
                ktm[:, tau * D + g * P: tau * D + g * P + 64],
                vtm[:, tau * D + g * P: tau * D + g * P + 64],
                start=True, stop=True), waits=(wpa if g == 0 else ()))
            B.op("T", lambda t, tau=tau, g=g: t.matmul(
                PSK[0][64:P, g * 64:(g + 1) * 64],
                ktm[:, tau * D + g * P + 64: tau * D + (g + 1) * P],
                vtm[:, tau * D + g * P + 64: tau * D + (g + 1) * P],
                start=True, stop=True))
            B.op("T", lambda t, tau=tau, g=g: t.matmul(
                PSK[1][:, g:g + 1],
                ktm[:, tau * D + g * P: tau * D + (g + 1) * P],
                onescol[:, :], start=True, stop=True),
                inc=("T" if g == NG - 1 else None))
        t_pamm = ("T", B.count["T"])
        wsa = [t_v_padd[tau - 1]] if tau >= 1 else [t_srun0]
        for g in range(NG):
            tpa = B.op("A", lambda a, tau=tau, g=g: a.copy(
                spref_tmp[:, g * 1024 + tau * 64: g * 1024 + (tau + 1) * 64],
                srun[:, g * 64:(g + 1) * 64]),
                waits=(wsa if g == 0 else ()), inc=("A" if g == NG - 1 else None))
        for g in range(NG):
            B.op("V", lambda v, g=g: v.tensor_add(
                srun[:, g * 64:(g + 1) * 64], srun[:, g * 64:(g + 1) * 64],
                PSK[0][:, g * 64:(g + 1) * 64]),
                waits=([t_pamm, tpa] if g == 0 else ()))
        t_v_padd[tau] = B.op("V", lambda v, tau=tau: v.tensor_copy(
            csums[:, tau * 8:(tau + 1) * 8], PSK[1][:, 0:8]), inc="V")

    t_srun0 = B.op("G", lambda g: g.memset(srun[:, :], 0.0), inc="G")
    for tau in range(3):
        ldtok[tau] = dma_x(tau, xin[tau])
    emit_ln1(0)
    emit_ln1(1)

    for tau in range(NT):
        p = tau % 2
        wt = [t_norm[tau]]
        if tau == 0:
            wt.append(t_cvals)
        if tau >= 1:
            wt.append(t_stripe[tau - 1])
            wt.append(t_stripea[tau - 1])
        tt = None
        for g in range(8):
            tt = B.op("T", lambda t, g=g, p=p: t.transpose(
                PST[g % 2][:, (g // 2) * P:(g // 2 + 1) * P],
                xhat[p][:, g * P:(g + 1) * P], ident[:, :]),
                waits=(wt if g == 0 else ()), inc=("T" if g == 7 else None))
        t_tr[tau] = tt
        ws = [t_tr[tau]]
        if tau >= 2:
            ws.append(t_strip_cons[tau - 2])
        sview = strip[p][:, :].rearrange("p (q r b) -> p r q b", r=2, b=P)
        t_stripea[tau] = B.op("A", lambda a, sview=sview: a.copy(
            sview[:, 1, :, :], PST[1][:, 0:512].rearrange("p (q b) -> p q b", b=P)),
            waits=ws, inc="A")
        t_stripe[tau] = B.op("V", lambda v, sview=sview: v.tensor_copy(
            sview[:, 0, :, :], PST[0][:, 0:512].rearrange("p (q b) -> p q b", b=P)),
            waits=ws, inc="V")
        if tau + 2 < NT:
            emit_ln1(tau + 2)
        if tau + 3 < NT:
            ldtok[tau + 3] = dma_x(tau + 3, xin[(tau + 3) % 3], waits=[t_norm[tau]])
        if tau >= 1:
            emit_pa(tau - 1)
        banks6 = [(PSQ[0], 0), (PSQ[1], 512), (PSV[0], 2048),
                  (PSV[1], 2560), (PSK[0], 1024), (PSK[1], 1536)]
        wmm = [t_stripe[tau], t_stripea[tau]]
        if tau >= 1:
            wmm += [t_pq[tau - 1], t_pk[tau - 1], t_pv[tau - 1]]
        for c in range(8):
            for j in range(6):
                bank, off = banks6[j]
                wq = list(wmm) if (c == 0 and j == 0) else []
                if c == 0 and j == 4 and tau >= 1:
                    wq.append(t_v_padd[tau - 1])
                if tau == 0 and j == 0:
                    wq.append(wq_tok[c])
                B.op("T", lambda t, p=p, c=c, bank=bank, off=off: t.matmul(
                    bank[:, :], strip[p][:, c * P:(c + 1) * P],
                    wreg[:, c * 3072 + off: c * 3072 + off + 512],
                    start=(c == 0), stop=(c == 7)),
                    waits=wq, inc=("T" if (c == 7 and j == 5) else None))
        tmm = ("T", B.count["T"])
        t_strip_cons[tau] = tmm
        qeb = [qe, qe2][tau % 2]
        wq2 = [tmm]
        if tau >= 2:
            wq2.append(t_qe_done[tau - 2])
        B.op("A", lambda a, tau=tau: a.activation(
            ktm[:, tau * D: tau * D + 512], PSK[0][:, :], AF.Exp), waits=wq2)
        t_pk[tau] = B.op("A", lambda a, tau=tau: a.activation(
            ktm[:, tau * D + 512: (tau + 1) * D], PSK[1][:, :], AF.Exp), inc="A")
        B.op("A", lambda a, tau=tau: a.copy(
            vtm[:, tau * D: tau * D + 512], PSV[0][:, :]))
        t_pv[tau] = B.op("A", lambda a, tau=tau: a.copy(
            vtm[:, tau * D + 512: (tau + 1) * D], PSV[1][:, :]), inc="A")
        B.op("A", lambda a, qeb=qeb: a.activation(qeb[:, 0:512], PSQ[0][:, :], AF.Exp))
        t_pq[tau] = B.op("A", lambda a, qeb=qeb: a.activation(
            qeb[:, 512:1024], PSQ[1][:, :], AF.Exp), inc="A")
        if tau >= 1:
            emit_softmax(tau - 1)
    emit_pa(NT - 1)
    emit_softmax(NT - 1)

    t_p12_T = ("T", B.count["T"])
    t_p12_A = ("A", B.count["A"])
    if limit == "p12":
        return _close_early([qsm[:, 0:1024], ktm[:, 0:1024], vtm[:, 0:1024],
                             qsm[:, 15 * 1024:16 * 1024], ktm[:, 15 * 1024:16 * 1024]])

    wout_tok = [dma(wout[:, c * 1024:(c + 1) * 1024], woutT[c * P:(c + 1) * P, :],
                    sem="dWo", waits=([t_p12_T] if c == 0 else ()))
                for c in range(8)]
    wout_tok = [wout_tok[-1]]

    prev_cz = None
    for g in range(NG):
        wv = [t_v_padd[NT - 1], t_stripe[NT - 1], t_stripea[NT - 1],
              t_pk[NT - 1]]
        if prev_cz is not None:
            wv.append(prev_cz)
        tcz = B.op("V", lambda v, g=g: v.tensor_copy(
            cz_sb[:, :],
            csums[:, :].rearrange("p (i g) -> p g i", g=NG)[:, g, :]),
            waits=wv, inc="V")
        czbank = PST[g % 2]
        ttr = B.op("T", lambda t, czbank=czbank: t.transpose(
            czbank[0:16, 0:P], cz_sb[:, :], ident[:, :]), waits=[tcz], inc="T")
        tev = B.op("V", lambda v, czbank=czbank: v.tensor_copy(
            czT_sb[:, :], czbank[0:16, 0:P]), waits=[ttr], inc="V")
        tmm = B.op("T", lambda t: t.matmul(PS1b[:, 0:17], czT_sb[:, :], tri16[:, :],
                                           start=True, stop=True), waits=[tev], inc="T")
        prev_cz = B.op("V", lambda v, g=g: v.tensor_scalar_add(
            zpref[:, g * 17:(g + 1) * 17], PS1b[:, 0:17], EPS_Z), waits=[tmm], inc="V")

    B.op("V", lambda v: v.tensor_scalar_mul(blob_s[:, 0:512], srun[:, :], smask[:, 0:1]),
         waits=[prev_cz, t_consts])
    tb2 = B.op("V", lambda v: v.tensor_scalar_mul(
        blob_s[:, 512:520],
        zpref[:, :].rearrange("p (g s) -> p g s", s=17)[:, :, 16], smask[:, 0:1]),
        inc="V")
    tsend = B.op("G", lambda g: g.dma_start(out=cc_in.ap(), in_=blob_s[:, :]),
                 waits=[tb2], inc="dWs")
    B.op("G", lambda g: g.collective_compute(
        "AllReduce", ALU.add, replica_groups=[[0, 1], [2, 3], [4, 5], [6, 7]],
        ins=[cc_in.ap().opt()], outs=[cc_out.ap().opt()]), waits=[tsend], inc="cc")
    trecv = B.op("G", lambda g: g.dma_start(out=brecv[:, :], in_=cc_out.ap()),
                 waits=[("cc", 1), t_p12_T], inc="dWr")

    t_pa_T = ("T", B.count["T"])
    t_pa_V = ("V", B.count["V"])
    t_pa_A = ("A", B.count["A"])
    if limit == "c2":
        return _close_early([spref[:, 0:1024], zpref[:, :], srun[:, :], brecv[:, :]])

    relZ = [t_pa_A, t_pa_A]
    relO = [t_pa_V, t_pa_V]
    relA_ = t_pa_V
    relI = [t_pa_A, t_pa_A]
    rel_tr_v = t_pa_V
    rel_tr_a = t_pa_A
    rel_qt = [t_pa_T, t_pa_T]
    rel_kc = [t_pa_T, t_pa_T]
    rel_ab = [t_pa_T, t_pa_T]
    rel_rt = [("G", B.count.get("G", 0))] * 2
    rel_tmpO = [t_pa_V, t_pa_V]

    nunits = NT * 4
    if limit.startswith("pbu"):
        nunits = int(limit[3:])

    ta_r_arr = [None, None]
    tv1_a = [None, None]
    ta1_a = [None, None]
    tg2_a = [None, None]
    tt2_a = [None, None]

    def st_tr(u):
        i, gg = u // 4, u % 4
        gA, gB = 2 * gg, 2 * gg + 1
        c0 = (u % 2) * 256
        nonlocal_w = [rel_tr[0], rel_tr[1]]
        B.op("T", lambda t, i=i, gA=gA, c0=c0: t.transpose(
            PS2[:, c0:c0 + P], qsm[:, i * D + gA * P: i * D + (gA + 1) * P],
            ident[:, :]), waits=nonlocal_w)
        B.op("T", lambda t, i=i, gA=gA, c0=c0: t.transpose(
            PS2b[:, c0:c0 + P], ktm[:, i * D + gA * P: i * D + (gA + 1) * P],
            ident[:, :]))
        B.op("T", lambda t, i=i, gB=gB, c0=c0: t.transpose(
            PS2[:, c0 + P:c0 + 256], qsm[:, i * D + gB * P: i * D + (gB + 1) * P],
            ident[:, :]))
        tt2_a[u % 2] = B.op("T", lambda t, i=i, gB=gB, c0=c0: t.transpose(
            PS2b[:, c0 + P:c0 + 256], ktm[:, i * D + gB * P: i * D + (gB + 1) * P],
            ident[:, :]), inc="T")

    def st_evac(u):
        pp = u % 2
        c0 = (u % 2) * 256
        wv1 = [tt2_a[pp], rel_qt[pp]]
        wa1 = [tt2_a[pp], rel_kc[pp]]
        if u == 0:
            wv1.append(t_consts)
            wa1.append(t_consts)
        tv1_a[pp] = B.op("V", lambda v, pp=pp, c0=c0: v.tensor_copy(
            qt[pp][:, :], PS2[:, c0:c0 + 256]), waits=wv1, inc="V")
        ta1_a[pp] = B.op("A", lambda a, pp=pp, c0=c0: a.copy(
            kc[pp][:, :], PS2b[:, c0:c0 + 256]), waits=wa1, inc="A")
        rel_tr[0] = tv1_a[pp]
        rel_tr[1] = ta1_a[pp]

    def st_zr(u, first=False):
        i, gg = u // 4, u % 4
        gA, gB = 2 * gg, 2 * gg + 1
        pz = u % 2
        PSzu = [PS0, PS0b][pz]
        B.op("T", lambda t, PSzu=PSzu, i=i, gA=gA: t.matmul(
            PSzu[:, 0:P], ktm[:, i * D + gA * P: i * D + (gA + 1) * P],
            maskM[:, :], start=True, stop=True), waits=[relZ[pz], relOz[pz]])
        tz = B.op("T", lambda t, PSzu=PSzu, i=i, gB=gB: t.matmul(
            PSzu[:, P:256], ktm[:, i * D + gB * P: i * D + (gB + 1) * P],
            maskM[:, :], start=True, stop=True), inc="T")
        wr = [tz, rel_rt[pz]]
        if first:
            wr.append(t_merge)
        B.op("A", lambda a, pz=pz, PSzu=PSzu, g=gA, i=i: a.activation(
            rt[pz][:, 0:P], PSzu[:, 0:P], AF.Ln,
            bias=zpref[:, g * 17 + i: g * 17 + i + 1]), waits=wr)
        B.op("A", lambda a, pz=pz, PSzu=PSzu, g=gB, i=i: a.activation(
            rt[pz][:, P:256], PSzu[:, P:256], AF.Ln,
            bias=zpref[:, g * 17 + i: g * 17 + i + 1]))
        ta_r_arr[pz] = B.op("A", lambda a, pz=pz: a.activation(
            rt[pz][:, 0:256], rt[pz][:, 0:256], AF.Exp, scale=-1.0), inc="A")
        relZ[pz] = ta_r_arr[pz]

    def st_qmul(u):
        pp = u % 2
        tg2_a[pp] = B.op("G", lambda gp, pp=pp: gp.tensor_mul(
            qt[pp][:, :], qt[pp][:, :], rt[pp][:, :]),
            waits=[tv1_a[pp], ta_r_arr[pp], rel_rt[pp]], inc="G")
        rel_rt[pp] = tg2_a[pp]

    t_xo_last = None

    def st_core(u):
        nonlocal t_xo_last
        global_ = None
        i, gg = u // 4, u % 4
        gA, gB = 2 * gg, 2 * gg + 1
        pp = u % 2
        PSzi = [PS0, PS0b][(u + 1) % 2]
        PSi = [PS3, PS3b][pp]
        c0 = (u % 2) * 256
        B.op("T", lambda t, pp=pp, c0=c0: t.matmul(
            PS1[:, c0:c0 + P], kc[pp][0:64, 0:P], qt[pp][0:64, 0:P],
            start=True, stop=True), waits=[ta1_a[pp], tg2_a[pp], relA_[0]])
        B.op("T", lambda t, pp=pp, c0=c0: t.matmul(
            PS1b[:, c0:c0 + P], kc[pp][64:P, 0:P], qt[pp][64:P, 0:P],
            start=True, stop=True))
        B.op("T", lambda t, pp=pp, c0=c0: t.matmul(
            PS1[:, c0 + P:c0 + 256], kc[pp][0:64, P:256], qt[pp][0:64, P:256],
            start=True, stop=True))
        tA = B.op("T", lambda t, pp=pp, c0=c0: t.matmul(
            PS1b[:, c0 + P:c0 + 256], kc[pp][64:P, P:256], qt[pp][64:P, P:256],
            start=True, stop=True), inc="T")
        rel_kc[pp] = tA
        B.op("V", lambda v, pp=pp, c0=c0: v.tensor_mul(
            ab[pp][:, 0:256], PS1[:, c0:c0 + 256], maskM2[:, :]),
            waits=[tA, rel_ab[pp]])
        tv4 = B.op("V", lambda v, pp=pp, c0=c0: v.tensor_mul(
            ab[pp][:, 256:512], PS1b[:, c0:c0 + 256], maskM2[:, :]), inc="V")
        relA_[0] = tv4
        B.op("T", lambda t, pp=pp, PSi=PSi, i=i, gA=gA: t.matmul(
            PSi[0:64, 0:P], vtm[:, i * D + gA * P: i * D + gA * P + 64],
            ab[pp][:, 0:P], start=True, stop=True), waits=[tv4, relI[pp]])
        B.op("T", lambda t, pp=pp, PSi=PSi, i=i, gA=gA: t.matmul(
            PSi[64:P, 0:P], vtm[:, i * D + gA * P + 64: i * D + (gA + 1) * P],
            ab[pp][:, 256:384], start=True, stop=True))
        B.op("T", lambda t, pp=pp, PSi=PSi, i=i, gB=gB: t.matmul(
            PSi[0:64, P:256], vtm[:, i * D + gB * P: i * D + gB * P + 64],
            ab[pp][:, P:256], start=True, stop=True))
        tI = B.op("T", lambda t, pp=pp, PSi=PSi, i=i, gB=gB: t.matmul(
            PSi[64:P, P:256], vtm[:, i * D + gB * P + 64: i * D + (gB + 1) * P],
            ab[pp][:, 384:512], start=True, stop=True), inc="T")
        rel_ab[pp] = tI
        tvo = B.op("V", lambda v, pp=pp, PSi=PSi: v.tensor_copy(
            tmpO[pp][:, :], PSi[:, 0:256]), waits=[tI, rel_tmpO[pp]], inc="V")
        relI[pp] = tvo
        B.op("T", lambda t, PSzi=PSzi, pp=pp, i=i, gA=gA: t.matmul(
            PSzi[0:64, 256:384],
            spref[0:64, gA * 1024 + i * 64: gA * 1024 + (i + 1) * 64],
            qt[pp][0:64, 0:P], start=True, stop=True),
            waits=[tg2_a[pp], relZ[(u + 1) % 2], relOz[(u + 1) % 2]])
        B.op("T", lambda t, PSzi=PSzi, pp=pp, i=i, gA=gA: t.matmul(
            PSzi[64:P, 256:384],
            spref[64:P, gA * 1024 + i * 64: gA * 1024 + (i + 1) * 64],
            qt[pp][64:P, 0:P], start=True, stop=True))
        B.op("T", lambda t, PSzi=PSzi, pp=pp, i=i, gB=gB: t.matmul(
            PSzi[0:64, 384:512],
            spref[0:64, gB * 1024 + i * 64: gB * 1024 + (i + 1) * 64],
            qt[pp][0:64, P:256], start=True, stop=True))
        tJ = B.op("T", lambda t, PSzi=PSzi, pp=pp, i=i, gB=gB: t.matmul(
            PSzi[64:P, 384:512],
            spref[64:P, gB * 1024 + i * 64: gB * 1024 + (i + 1) * 64],
            qt[pp][64:P, P:256], start=True, stop=True), inc="T")
        rel_qt[pp] = tJ
        xodst = xo[:, :].rearrange("p (g t) -> p g t", t=2048)[
            :, gA:gB + 1, i * P:(i + 1) * P]
        t_xo_last = tv5 = B.op("V", lambda v, PSzi=PSzi, pp=pp, xodst=xodst: v.tensor_add(
            xodst, tmpO[pp][:, :].rearrange("p (g t) -> p g t", t=P),
            PSzi[:, 256:512].rearrange("p (g t) -> p g t", t=P)),
            waits=[tJ], inc="V")
        relOz[(u + 1) % 2] = tv5

    relZ = [t_pa_A, t_pa_A]
    relOz = [t_pa_V, t_pa_V]
    relA_ = [t_pa_V]
    relI = [t_pa_A, t_pa_A]
    rel_tr = [t_pa_V, t_pa_A]
    rel_qt = [t_pa_T, t_pa_T]
    rel_kc = [t_pa_T, t_pa_T]
    rel_ab = [t_pa_T, t_pa_T]
    rel_rt = [("G", B.count.get("G", 0))] * 2
    rel_tmpO = [t_pa_V, t_pa_V]

    st_tr(0)
    st_evac(0)
    if nunits > 1:
        st_tr(1)
        st_evac(1)
    B.op("V", lambda v: v.tensor_scalar_mul(rqb[:, 0:8], brecv[:, 512:520],
                                            rmask[:, 0:1]), waits=[trecv])
    for g in range(NG):
        B.op("V", lambda v, g=g: v.tensor_scalar_add(
            zpref[:, g * 17:(g + 1) * 17], zpref[:, g * 17:(g + 1) * 17],
            rqb[:, g:g + 1]))
    for i in range(NT):
        tok = B.op("V", lambda v, i=i: v.scalar_tensor_tensor(
            spref[:, :].rearrange("p (g i d) -> p g i d", i=NT, d=64)[:, :, i, :],
            brecv[:, 0:512].rearrange("p (g d) -> p g d", d=64), rmask[:, 0:1],
            spref_tmp[:, :].rearrange("p (g i d) -> p g i d", i=NT, d=64)[:, :, i, :],
            ALU.mult, ALU.add))
    t_merge = tok

    st_zr(0, first=True)
    if nunits > 1:
        st_zr(1)
    st_qmul(0)
    for u in range(nunits):
        if 1 < u + 1 < nunits:
            st_tr(u + 1)
            st_evac(u + 1)
        if u + 2 < nunits:
            st_zr(u + 2)
        if u + 1 < nunits:
            st_qmul(u + 1)
        st_core(u)

    t_pb_T = ("T", B.count["T"])
    t_pb_V = ("V", B.count["V"])
    t_pb_A = ("A", B.count["A"])
    t_pb_G = ("G", B.count["G"])
    if limit == "pb" or limit.startswith("pbu") or limit.startswith("pbo"):
        return _close_early([xo[:, 0:1024], xo[:, 15 * 1024:16 * 1024]])

    dma(lorab_sb[:, :], lraB, sem="dWf", waits=[t_pb_G])
    dma(rw_sb[:, :].rearrange("p (c o) -> p c o", c=8),
        rw_h.rearrange("(c p) o -> p c o", p=P), sem="dWf")
    t_wffn_small = dma(bb_bc[:, :], bb_h.partition_broadcast(P), sem="dWf")
    wbase_tok = [dma(wbase[:, c * 1024:(c + 1) * 1024], baseT[c * P:(c + 1) * P, :],
                     sem="dWb2")
                 for c in range(8)]
    wbase_tok = [wbase_tok[-1]]

    def x2slice(tau):
        if tau < 8:
            return x2a[:, tau * D:(tau + 1) * D]
        return x2b[:, (tau - 8) * D:(tau - 7) * D]

    ld4 = {}
    ld4[0] = dma_x(0, xin[0], waits=[t_pb_V, t_pb_T, t_pb_G, t_pb_A])
    ld4[1] = dma_x(1, xin[1])
    relPQ4 = [t_pb_V, t_pb_V]
    t_v_x2 = {}
    tae4 = {}

    def p4_epilogue(tau):
        par = tau % 3
        cols = cols4v[par]
        B.op("V", lambda v, cols=cols: v.tensor_add(cols[:, 0:1], cols[:, 0:1],
                                                    cols[:, 6:7]), waits=[tae4[tau]])
        B.op("V", lambda v, par=par, cols=cols: v.scalar_tensor_tensor(
            junk[:, :], a_sb[par][:, :], 1.0, a_sb[par][:, :], ALU.mult, ALU.mult,
            accum_out=cols[:, 1:2]))
        tb = ln_stats(cols)
        ta2 = B.op("A", lambda a, par=par, cols=cols: a.activation(
            t1[:, :], a_sb[par][:, :], AF.Identity, bias=cols[:, 4:5],
            scale=cols[:, 7:8]), waits=[tb], inc="A")
        wv = [ta2, ld4[tau]]
        if tau == 0:
            wv.append(t_pb_T)
        t_v_x2[tau] = B.op("V", lambda v, tau=tau: v.tensor_add(
            x2slice(tau), t1[:, :], xin[tau % 2][:, :]), waits=wv, inc="V")
        if tau + 2 < NT:
            ld4[tau + 2] = dma_x(tau + 2, xin[tau % 2], waits=[t_v_x2[tau]])

    for tau in range(NT):
        par = tau % 3
        PSo = [[PS0, PS0b], [PS1, PS1b]][tau % 2]
        cols = cols4v[par]
        wt = [relPQ4[tau % 2]]
        if tau == 0:
            wt += [t_xo_last] + wout_tok
        for c in range(8):
            for g2 in range(2):
                B.op("T", lambda t, g2=g2, c=c, tau=tau, PSo=PSo: t.matmul(
                    PSo[g2][:, :], xo[:, c * 2048 + tau * P: c * 2048 + (tau + 1) * P],
                    wout[:, c * 1024 + g2 * 512: c * 1024 + (g2 + 1) * 512],
                    start=(c == 0), stop=(c == 7)),
                    waits=(wt if (g2 == 0 and c == 0) else ()),
                    inc=("T" if (g2 == 1 and c == 7) else None))
        tmm = ("T", B.count["T"])
        wa = [tmm]
        if tau >= 3:
            wa.append(t_v_x2[tau - 3])
        B.op("A", lambda a, par=par, PSo=PSo, cols=cols: a.activation(
            a_sb[par][:, 0:512], PSo[0][:, :], AF.Copy, accum_out=cols[:, 0:1]),
            waits=wa)
        tae4[tau] = ta = B.op("A", lambda a, par=par, PSo=PSo, cols=cols: a.activation(
            a_sb[par][:, 512:1024], PSo[1][:, :], AF.Copy, accum_out=cols[:, 6:7]),
            inc="A")
        relPQ4[tau % 2] = ta
        if tau >= 1:
            p4_epilogue(tau - 1)
    p4_epilogue(NT - 1)

    t_p4_V = ("V", B.count["V"])
    if limit == "p4":
        return _close_early([x2a[:, 0:1024], x2b[:, 7 * 1024:8 * 1024]])

    relPF = [t_p4_V, t_p4_V]
    relPR = [t_p4_V, t_p4_V]
    t_norm3 = {}
    t_tr3 = {}
    t_stripe3 = {}
    t_stripe3a = {}
    t_ctev = {}
    t_cb = {}
    t_ct_cons = {}
    t_strip_cons3 = {}
    t_x3 = {}
    tff = {}

    t_ln3_tb = {}

    def emit_ln3(tau):
        par = tau % 2
        cols = cols3v[par]
        xsl = x2slice(tau)
        wv = [t_v_x2[tau]]
        B.op("V", lambda v, xsl=xsl, cols=cols: v.reduce_sum(cols[:, 0:1], xsl, axis=AX.X),
             waits=wv)
        wg = [t_v_x2[tau]]
        if tau >= 2:
            wg.append(t_ln3_tb[tau - 2])
        if tau < 2:
            wg.append(t_pb_T)
        tg = B.op("A", lambda a, xsl=xsl, cols=cols: a.activation(
            junk2[:, :], xsl, AF.Square, accum_out=cols[:, 1:2]),
            waits=wg, inc="A")
        tb = ln_stats(cols, w_in=[tg])
        t_ln3_tb[tau] = tb
        wn = [tb]
        if tau >= 2:
            wn.append(t_tr3[tau - 2])
        t_norm3[tau] = B.op("A", lambda a, xsl=xsl, cols=cols, par=par: a.activation(
            xhat[par][:, :], xsl, AF.Identity, bias=cols[:, 4:5], scale=cols[:, 7:8]),
            waits=wn, inc="A")

    def p56_epilogue(tau):
        par = tau % 3
        cols = cols4v[par]
        PSf = [[PS0, PS0b], [PS1, PS1b]][tau % 2]
        xsl = x2slice(tau)
        wv2 = [tff[tau]]
        if tau >= 3:
            wv2.append(t_x3[tau - 3])
        if tau == 0:
            wv2.append(t_wffn_small)
        B.op("V", lambda v, par=par, PSf=PSf, cols=cols: v.scalar_tensor_tensor(
            a_sb[par][:, 0:512], PSf[0][:, :], 1.0, bb_bc[:, 0:512], ALU.mult, ALU.add,
            accum_out=cols[:, 0:1]), waits=wv2)
        tv3 = B.op("V", lambda v, par=par, PSf=PSf, cols=cols: v.scalar_tensor_tensor(
            a_sb[par][:, 512:1024], PSf[1][:, :], 1.0, bb_bc[:, 512:1024], ALU.mult,
            ALU.add, accum_out=cols[:, 3:4]), inc="V")
        relPF[tau % 2] = tv3
        B.op("V", lambda v, cols=cols: v.tensor_add(cols[:, 0:1], cols[:, 0:1],
                                                    cols[:, 3:4]))
        tsq = B.op("A", lambda a, par=par, cols=cols: a.activation(
            junk[:, :], a_sb[par][:, :], AF.Square, accum_out=cols[:, 1:2]),
            waits=[tv3], inc="A")
        tb2 = ln_stats(cols, w_in=[tsq])
        ta2 = B.op("A", lambda a, par=par, cols=cols: a.activation(
            t1[:, :], a_sb[par][:, :], AF.Identity, bias=cols[:, 4:5],
            scale=cols[:, 7:8]), waits=[tb2], inc="A")
        t_x3[tau] = B.op("V", lambda v, xsl=xsl: v.tensor_add(xsl, xsl, t1[:, :]),
                         waits=[ta2], inc="V")
        B.op("S", lambda s, tau=tau, xsl=xsl: s.dma_start(
            out=o_hbm[tau * P:(tau + 1) * P, :], in_=xsl), waits=[t_x3[tau]],
            inc="dOut")

    emit_ln3(0)
    emit_ln3(1)
    for tau in range(NT):
        par = tau % 2
        cols = colsv[par]
        PSf = [[PS0, PS0b], [PS1, PS1b]][par]
        PSr = [PS3, PS3b][par]
        PSt = [PS2, PS2b][par]
        wt = [t_norm3[tau]]
        if tau >= 1:
            wt.append(t_stripe3[tau - 1])
            wt.append(t_stripe3a[tau - 1])
            wt.append(t_ctev[tau - 1])
        else:
            wt += [t_pb_V, t_pb_A]
        tt = None
        for g in range(8):
            tt = B.op("T", lambda t, g=g, par=par, tau=tau: t.transpose(
                PST[(g + tau) % 2][:, (g // 2) * P:(g // 2 + 1) * P],
                xhat[par][:, g * P:(g + 1) * P], ident[:, :]),
                waits=(wt if g == 0 else ()), inc=("T" if g == 7 else None))
        t_tr3[tau] = tt
        ws = [t_tr3[tau]]
        if tau >= 2:
            ws.append(t_strip_cons3[tau - 2])
        else:
            ws.append(t_pb_G)
        sview = strip[par][:, :].rearrange("p (q r b) -> p r q b", r=2, b=P)
        t_stripe3a[tau] = B.op("A", lambda a, sview=sview, tau=tau: a.copy(
            sview[:, 1 - tau % 2, :, :],
            PST[1][:, 0:512].rearrange("p (q b) -> p q b", b=P)), waits=ws, inc="A")
        t_stripe3[tau] = B.op("A", lambda a, sview=sview, tau=tau: a.copy(
            sview[:, tau % 2, :, :], PST[0][:, 0:512].rearrange("p (q b) -> p q b", b=P)),
            waits=ws, inc="A")
        wr = [t_stripe3[tau], t_stripe3a[tau],
              (t_cb[tau - 2] if tau >= 2 else relPR[par])]
        if tau == 0:
            wr.append(t_wffn_small)
        for c in range(8):
            B.op("T", lambda t, c=c, par=par, PSr=PSr: t.matmul(
                PSr[:, 0:16], strip[par][:, c * P:(c + 1) * P],
                rw_sb[:, c * 16:(c + 1) * 16], start=(c == 0), stop=(c == 7)),
                waits=(wr if c == 0 else ()), inc=("T" if c == 7 else None))
        tr = ("T", B.count["T"])
        wb = [relPF[par]]
        if tau == 0:
            wb += wbase_tok
        for c in range(8):
            for g2 in range(2):
                t_base = B.op("T", lambda t, g2=g2, c=c, par=par, PSf=PSf: t.matmul(
                    PSf[g2][:, :], strip[par][:, c * P:(c + 1) * P],
                    wbase[:, c * 1024 + g2 * 512: c * 1024 + (g2 + 1) * 512],
                    start=(c == 0), stop=False),
                    waits=(wb if (g2 == 0 and c == 0) else ()),
                    inc=("T" if (c == 7 and g2 == 1) else None))
        t_strip_cons3[tau] = t_base
        wlg = [tr]
        if tau == 0:
            wlg.append(t_consts)
        tv = B.op("V", lambda v, PSr=PSr: v.tensor_add(lgb[:, :], PSr[:, 0:8], rb_bc[:, :]),
                  waits=wlg, inc="V")
        ta = B.op("A", lambda a: a.activation(geb[:, :], lgb[:, :], AF.Exp,
                                              accum_out=gsb[:, :]), waits=[tv], inc="A")
        B.op("V", lambda v: v.reciprocal(gqb[:, :], gsb[:, :]), waits=[ta])
        tv2 = B.op("V", lambda v: v.tensor_scalar_mul(gbuf[:, :], geb[:, :], gqb[:, 0:1]))
        tv2 = B.op("V", lambda v: v.tensor_copy(gbuf32[:, :], gbuf[:, :]), inc="V")
        wc = [tv2]
        if tau >= 2:
            wc.append(t_ct_cons[tau - 2])
        tcb = None
        for e in range(8):
            tcb = B.op("V", lambda v, e=e, par=par, PSr=PSr: v.tensor_scalar(
                cbuf[par][:, e * 8:(e + 1) * 8], PSr[:, 8:16], gbuf32[:, e:e + 1], 2.0,
                ALU.mult, ALU.mult), waits=(wc if e == 0 else ()),
                inc=("V" if e == 7 else None))
        t_cb[tau] = tcb
        cslot = PST[(tau + 1) % 2][0:64, 512:512 + P]
        wct = [tcb]
        if tau >= 2:
            wct.append(t_ctev[tau - 2])
        tct = B.op("T", lambda t, cslot=cslot, par=par: t.transpose(
            cslot, cbuf[par][:, :], ident[:, :]), waits=wct, inc="T")
        t_ctev[tau] = tce = B.op("V", lambda v, cslot=cslot, par=par: v.tensor_copy(
            ctsb[par][:, :], cslot), waits=[tct], inc="V")
        for g2 in range(2):
            tff[tau] = B.op("T", lambda t, g2=g2, par=par, PSf=PSf: t.matmul(
                PSf[g2][:, :], ctsb[par][:, :],
                lorab_sb[:, g2 * 512:(g2 + 1) * 512], start=False, stop=True),
                waits=([tce] if g2 == 0 else ()),
                inc=("T" if g2 == 1 else None))
        t_ct_cons[tau] = tff[tau]
        relPR[par] = t_cb[tau]
        if tau + 2 < NT:
            emit_ln3(tau + 2)
        if tau >= 1:
            p56_epilogue(tau - 1)
    p56_epilogue(NT - 1)

    B.op("S", lambda s: s.nop(), waits=[("dOut", 16 * NT)])

    from contextlib import ExitStack
    sem_names = sorted(DMA_SEMS) + ["cc", "T", "V", "A", "G"]
    with ExitStack() as stack:
        sems = {n: stack.enter_context(nc.semaphore(f"sem_{n}")) for n in sem_names}
        blk = stack.enter_context(nc.Block())
        B.replay(blk, sems)
    return nc


_NC_CACHE = None


def kernel(L0_attn_x, L0_attn_W_qkv, L0_attn_W_out, L0_ffn_base_w, L0_ffn_base_b,
           L0_ffn_lora_A, L0_ffn_lora_B, L0_ffn_router_w, L0_ffn_router_b):
    global _NC_CACHE
    x = np.ascontiguousarray(np.asarray(L0_attn_x, dtype=np.float32))
    wqkvT = np.ascontiguousarray(np.asarray(L0_attn_W_qkv, dtype=np.float32).T)
    woutT = np.ascontiguousarray(np.asarray(L0_attn_W_out, dtype=np.float32).T)
    baseT = np.ascontiguousarray(np.asarray(L0_ffn_base_w, dtype=np.float32).T)
    rw = np.ascontiguousarray(np.concatenate(
        [np.asarray(L0_ffn_router_w, dtype=np.float32).T,
         np.asarray(L0_ffn_lora_A, dtype=np.float32)], axis=1))
    lorab = np.ascontiguousarray(np.asarray(L0_ffn_lora_B, dtype=np.float32).reshape(64, 1024))
    rb = np.ascontiguousarray(np.asarray(L0_ffn_router_b, dtype=np.float32).reshape(1, 8))
    bb = np.ascontiguousarray(np.asarray(L0_ffn_base_b, dtype=np.float32).reshape(1, 1024))

    ident = np.eye(P, dtype=np.float32)
    s_idx = np.arange(P)[:, None]
    t_idx = np.arange(P)[None, :]
    maskM = (s_idx <= t_idx).astype(np.float32)
    maskM2 = np.concatenate([maskM, maskM], axis=1)
    tri16 = np.zeros((16, 17), dtype=np.float32)
    for j in range(16):
        tri16[j, j + 1:16] = 1.0
        tri16[j, 16] = 1.0
    ones_col = np.ones((P, 1), dtype=np.float32)
    EPS_LN_NP = 1e-5

    if _NC_CACHE is None:
        _NC_CACHE = build_nc()
    nc = _NC_CACHE

    in_maps = []
    for c in range(8):
        b, h = c // 2, c % 2
        send = 1.0 if h == 0 else 0.0
        in_maps.append({
            "x": np.ascontiguousarray(x[b, h * 2048:(h + 1) * 2048, :]),
            "wqkvT": wqkvT, "woutT": woutT, "baseT": baseT, "rw": rw,
            "loraB": lorab, "rb": rb, "bb": bb,
            "ident": ident, "maskM": maskM, "maskM2": maskM2, "tri16": tri16,
            "onescol": ones_col,
            "cvals": np.broadcast_to(np.array([EPS_LN_NP, -0.5], dtype=np.float32), (P, 2)).copy(),
            "smask": np.full((P, 1), send, dtype=np.float32),
            "rmask": np.full((P, 1), 1.0 - send, dtype=np.float32),
        })
    res = run_bass_kernel_spmd(nc, in_maps, core_ids=list(range(8)))
    out = np.empty((4, 4096, 1024), dtype=np.float32)
    for c in range(8):
        b, h = c // 2, c % 2
        out[b, h * 2048:(h + 1) * 2048, :] = res.results[c]["out"]
    return out


# version 19 (final)
# speedup vs baseline: 1.4088x; 1.1051x over previous
import numpy as np
import concourse.bass as bass
import concourse.mybir as mybir
from concourse.bass_utils import run_bass_kernel_spmd

F32 = mybir.dt.float32
BF16 = mybir.dt.bfloat16
AF = mybir.ActivationFunctionType
ALU = mybir.AluOpType
AX = mybir.AxisListType

P = 128
NT = 16
D = 1024
NG = 8
EPS_LN = 1e-5
EPS_Z = 1e-9

DMA_SEMS = {"dX0", "dX1", "dX2", "dOut", "dWc", "dWo", "dWb2", "dWf", "dWs", "dWr",
            "dWq0", "dWq1", "dWq2", "dWq3", "dWq4", "dWq5", "dWq6", "dWq7"}


class Builder:

    def __init__(self):
        self.sched = []
        self.count = {}

    def op(self, eng, fn, waits=(), inc=None, nodep=False):
        if eng in ("V", "A"):
            assert inc in (None, eng)
            inc = eng
        w = []
        for t in waits:
            if t is None:
                continue
            s, v = t
            assert v <= self.count.get(s, 0), f"wait on future: {s} {v}>{self.count.get(s,0)}"
            w.append((s, v))
        tok = None
        if inc is not None:
            amt = 16 if inc in DMA_SEMS else 1
            self.count[inc] = self.count.get(inc, 0) + amt
            tok = (inc, self.count[inc])
        self.sched.append((eng, w, fn, inc, nodep))
        return tok

    def replay(self, blk, sems):
        for eng_key, eng_name in [("S", "sync"), ("G", "gpsimd"), ("V", "vector"),
                                  ("A", "scalar"), ("T", "tensor")]:
            ops = [o for o in self.sched if o[0] == eng_key]
            serial = eng_key in ("V", "A", "G")

            def body(eng, ops=ops, eng_key=eng_key, serial=serial):
                last = {}
                self_idx = 0
                for (_e, waits, fn, inc, nodep) in ops:
                    if serial and self_idx > 0 and not nodep:
                        eng.wait_ge(sems[eng_key], self_idx)
                    for (s, v) in waits:
                        if s == eng_key:
                            continue
                        if last.get(s, -1) >= v:
                            continue
                        eng.wait_ge(sems[s], v)
                        last[s] = v
                    inst = fn(eng)
                    if inc is not None:
                        if inc == "cc":
                            inst.then_inc(sems[inc])
                        else:
                            inst.then_inc(sems[inc], 16 if inc in DMA_SEMS else 1)
                    if inc == eng_key:
                        self_idx += 1
            getattr(blk, eng_name)(body)


def build_nc(limit="all"):
    nc = bass.Bass("TRN2", target_bir_lowering=False, debug=False)

    x_hbm = nc.dram_tensor("x", [2048, D], F32, kind="ExternalInput").ap()
    wqkvT = nc.dram_tensor("wqkvT", [D, 3072], F32, kind="ExternalInput").ap()
    woutT = nc.dram_tensor("woutT", [D, D], F32, kind="ExternalInput").ap()
    baseT = nc.dram_tensor("baseT", [D, D], F32, kind="ExternalInput").ap()
    rw_h = nc.dram_tensor("rw", [D, 16], F32, kind="ExternalInput").ap()
    lraB = nc.dram_tensor("loraB", [64, D], F32, kind="ExternalInput").ap()
    rb_h = nc.dram_tensor("rb", [1, 8], F32, kind="ExternalInput").ap()
    bb_h = nc.dram_tensor("bb", [1, D], F32, kind="ExternalInput").ap()
    ident_h = nc.dram_tensor("ident", [P, P], F32, kind="ExternalInput").ap()
    maskM_h = nc.dram_tensor("maskM", [P, P], F32, kind="ExternalInput").ap()
    maskM2_h = nc.dram_tensor("maskM2", [P, 256], F32, kind="ExternalInput").ap()
    tri16_h = nc.dram_tensor("tri16", [16, 17], F32, kind="ExternalInput").ap()
    ones_h = nc.dram_tensor("onescol", [P, 1], F32, kind="ExternalInput").ap()
    smask_h = nc.dram_tensor("smask", [P, 1], F32, kind="ExternalInput").ap()
    rmask_h = nc.dram_tensor("rmask", [P, 1], F32, kind="ExternalInput").ap()
    cvals_h = nc.dram_tensor("cvals", [P, 2], F32, kind="ExternalInput").ap()
    o_hbm = nc.dram_tensor("out", [2048, D], F32, kind="ExternalOutput").ap()
    cc_in = nc.dram_tensor("cc_in", [P, 520], F32)
    cc_out = nc.dram_tensor("cc_out", [P, 520], F32)

    def alias(name, shape, dtype, base_handle, byte_off=0):
        h = nc.virtual_sbuf_tensor(name, shape, dtype)
        nc._tensor(h.name, list(h.shape), h.dtype, type="SB")
        mloc = nc.lookup_mloc(h)
        mloc.allocated = True
        mloc.addr = nc.lookup_mloc(base_handle).addr + byte_off
        return h

    alloc = nc.alloc_sbuf_tensor
    ident = alloc("identsb", [P, P], BF16)
    maskM = alloc("maskMsb", [P, P], BF16)
    maskM2 = alloc("maskM2sb", [P, 256], F32)
    tri16 = alloc("tri16sb", [16, 17], BF16)
    onescol = alloc("onescolsb", [P, 1], BF16)
    smask = alloc("smasksb", [P, 1], F32)
    rmask = alloc("rmasksb", [P, 1], F32)
    rb_bc = alloc("rb_bc", [P, 8], F32)
    cvals = alloc("cvalssb", [P, 2], F32)
    nc.const_aps.register(F32, EPS_LN, cvals[:, 0:1])
    nc.const_aps.register(F32, -0.5, cvals[:, 1:2])
    wreg = alloc("wreg", [P, 24576], BF16)
    qsm = alloc("qsm", [P, NT * D], BF16)
    ktm = alloc("ktm", [P, NT * D], BF16)
    vtm = alloc("vtm", [P, NT * D], BF16)
    xo = alloc("xo", [P, NG * 2048], BF16)
    zpref = alloc("zpref", [P, NG * 17], F32)
    srun = alloc("srun", [P, 512], F32)
    xin0 = alloc("xin0", [P, D], F32)
    xin1 = alloc("xin1", [P, D], F32)
    xhat0 = alloc("xhat0", [P, D], BF16)
    xhat1 = alloc("xhat1", [P, D], BF16)
    strip0 = alloc("strip0", [P, D], BF16)
    strip1 = alloc("strip1", [P, D], BF16)
    junk = alloc("junk", [P, D], F32)
    qe = alloc("qe", [P, D], F32)
    cols0 = alloc("cols0", [P, 8], F32)
    cols1 = alloc("cols1", [P, 8], F32)
    sseg = alloc("sseg", [P, 16], F32)
    rqb = alloc("rqbuf", [P, 16], F32)
    cz_sb = alloc("cz_sb", [P, 16], BF16)
    czT_sb = alloc("czT_sb", [16, P], BF16)

    xhat = [xhat0, xhat1]
    strip = [strip0, strip1]
    colsv = [cols0, cols1]

    x2a = alias("x2a", [P, 8 * D], F32, qsm)
    x2b = alias("x2b", [P, 8 * D], F32, ktm)
    wout = alias("wout", [P, 8192], BF16, wreg, 0)
    wbase = alias("wbase", [P, 8192], BF16, wreg, 16384)
    spref = alias("spref", [P, 8192], BF16, wreg, 32768)
    lorab_sb = alias("lorabsb", [64, D], BF16, wreg, 32768)
    rw_sb = alias("rw_sb", [P, 128], BF16, wreg, 32768 + 2048)
    bb_bc = alias("bb_bc", [P, D], F32, wreg, 32768 + 2048 + 256)
    t1 = alias("t1", [P, D], F32, junk)
    qe2 = alias("qe2", [P, D], F32, xo, 0)
    xin2 = alias("xin2", [P, D], F32, xo, 4096)
    xin = [xin0, xin1, xin2]
    blob_s = alias("blob_s", [P, 520], F32, xhat0)
    brecv = alias("brecv", [P, 520], F32, strip0)
    qt = [alias(f"qt{j}", [P, 256], BF16, xin0, j * 512) for j in range(2)]
    kc = [alias(f"kc{j}", [P, 256], BF16, xin0, 1024 + j * 512) for j in range(2)]
    ab = [alias(f"ab{j}", [P, 512], BF16, xin0, 2048 + j * 1024) for j in range(2)]
    rt = [alias(f"rt{j}", [P, 256], F32, xin1, j * 1024) for j in range(2)]
    tmpO = [alias(f"tmpO{j}", [P, 256], F32, xin1, 2048 + j * 1024) for j in range(2)]
    a_sb0 = alias("a_sb0", [P, D], F32, vtm, 0)
    a_sb1 = alias("a_sb1", [P, D], F32, vtm, 4096)
    cbuf = [alias(f"cbuf{j}", [P, 64], BF16, vtm, 8192 + j * 128) for j in range(2)]
    ctsb = [alias(f"ctsb{j}", [64, P], BF16, vtm, 8448 + j * 256) for j in range(2)]
    lgb = alias("lgb", [P, 8], F32, vtm, 8960)
    geb = alias("geb", [P, 8], F32, vtm, 8992)
    gsb = alias("gsb", [P, 1], F32, vtm, 9024)
    gqb = alias("gqb", [P, 1], F32, vtm, 9056)
    gbuf = alias("gbuf", [P, 8], BF16, vtm, 9088)
    gbuf32 = alias("gbuf32", [P, 8], F32, vtm, 9120)
    junk2 = alias("junk2", [P, D], F32, vtm, 12288)
    cols3v = [alias(f"cols3{j}", [P, 8], F32, vtm, 16384 + j * 32) for j in range(2)]
    a_sb = [a_sb0, a_sb1]

    PS0 = nc.alloc_psum_tensor("PS0", [P, 512], F32)
    PS0b = nc.alloc_psum_tensor("PS0b", [P, 512], F32)
    PS1 = nc.alloc_psum_tensor("PS1", [P, 512], F32)
    PS1b = nc.alloc_psum_tensor("PS1b", [P, 512], F32)
    PS2 = nc.alloc_psum_tensor("PS2", [P, 1024], BF16)
    PS2b = nc.alloc_psum_tensor("PS2b", [P, 1024], BF16)
    PS3 = nc.alloc_psum_tensor("PS3", [P, 512], F32)
    PS3b = nc.alloc_psum_tensor("PS3b", [P, 512], F32)
    PSQ = [PS0, PS0b]
    PSK = [PS1, PS1b]
    PSV = [PS3, PS3b]
    PST = [PS2, PS2b]

    B = Builder()

    def _close_early(aps=()):
        w = [(s, v) for s, v in B.count.items() if s in ("T", "V", "A", "G")]
        tok = B.op("G", lambda g: g.dma_start(out=o_hbm[0:P, :], in_=junk[:, :]),
                   waits=w, inc="dOut")
        for j, ap in enumerate(aps):
            tok = B.op("G", lambda g, ap=ap, j=j: g.dma_start(
                out=o_hbm[(j + 1) * P:(j + 2) * P, 0:ap.shape[-1]], in_=ap),
                inc="dOut")
        B.op("S", lambda s: s.nop(), waits=[tok])
        from contextlib import ExitStack
        sem_names = sorted(DMA_SEMS) + ["cc", "T", "V", "A", "G"]
        with ExitStack() as stack:
            sems = {n: stack.enter_context(nc.semaphore(f"sem_{n}")) for n in sem_names}
            blk = stack.enter_context(nc.Block())
            B.replay(blk, sems)
        return nc


    def dma(dst_ap, src_ap, sem="dWc", waits=()):
        return B.op("G", lambda g, d=dst_ap, s=src_ap: g.dma_start(out=d, in_=s),
                    waits=waits, inc=sem)

    dma(ident[:, :], ident_h)
    dma(maskM[:, :], maskM_h)
    dma(maskM2[:, :], maskM2_h)
    dma(tri16[:, :], tri16_h)
    dma(onescol[:, :], ones_h)
    dma(smask[:, :], smask_h)
    dma(rmask[:, :], rmask_h)
    dma(rb_bc[:, :], rb_h.partition_broadcast(P))
    t_consts = dma(cvals[:, :], cvals_h)
    B.op("A", lambda a: a.nop(), waits=[t_consts])
    wq_tok = [dma(wreg[:, c * 3072:(c + 1) * 3072], wqkvT[c * P:(c + 1) * P, :],
                  sem=f"dWq{c}") for c in range(8)]

    def dma_x(tau, buf, waits=()):
        return B.op("S", lambda s, t=tau, b=buf: s.dma_start(
            out=b[:, :], in_=x_hbm[t * P:(t + 1) * P, :]), waits=waits,
            inc=f"dX{tau % 3}")

    ldtok = {}

    def ln_stats(cols, w_in=()):
        B.op("V", lambda v, cols=cols: v.tensor_scalar_mul(cols[:, 2:3], cols[:, 0:1], -1.0 / D),
             waits=w_in)
        B.op("V", lambda v, cols=cols: v.tensor_scalar_mul(cols[:, 3:4], cols[:, 1:2], 1.0 / D))
        B.op("V", lambda v, cols=cols: v.tensor_mul(cols[:, 4:5], cols[:, 2:3], cols[:, 2:3]))
        tv = B.op("V", lambda v, cols=cols: v.tensor_sub(cols[:, 5:6], cols[:, 3:4], cols[:, 4:5]),
                  inc="V")
        B.op("A", lambda a, cols=cols: a.activation(cols[:, 6:7], cols[:, 5:6], AF.Ln,
                                                    bias=EPS_LN), waits=[tv])
        ta = B.op("A", lambda a, cols=cols: a.activation(cols[:, 7:8], cols[:, 6:7], AF.Exp,
                                                         scale=-0.5), inc="A")
        return B.op("V", lambda v, cols=cols: v.tensor_mul(cols[:, 4:5], cols[:, 2:3],
                                                           cols[:, 7:8]), waits=[ta], inc="V")

    t_norm = {}
    t_tr = {}
    t_stripe = {}
    t_strip_cons = {}
    t_qe_done = {}
    t_pq = {}
    t_pk = {}
    t_pv = {}

    def emit_ln1(tau):
        p3 = tau % 3
        p = tau % 2
        cols = colsv[p]
        xb = xin[p3]
        w0 = [ldtok[tau]]
        if tau >= 2:
            w0.append(t_tr[tau - 2])
        B.op("V", lambda v, xb=xb, cols=cols: v.reduce_sum(cols[:, 0:1], xb[:, :], axis=AX.X),
             waits=w0)
        B.op("V", lambda v, xb=xb, cols=cols: v.scalar_tensor_tensor(
            junk[:, :], xb[:, :], 1.0, xb[:, :], ALU.mult, ALU.mult,
            accum_out=cols[:, 1:2]))
        tb = ln_stats(cols)
        wn = [tb]
        if tau == 0:
            wn.append(t_consts)
        t_norm[tau] = B.op("A", lambda a, xb=xb, cols=cols, p=p: a.activation(
            xhat[p][:, :], xb[:, :], AF.Identity, bias=cols[:, 4:5], scale=cols[:, 7:8]),
            waits=wn, inc="A")

    def emit_softmax(tau):
        qeb = [qe, qe2][tau % 2]
        B.op("V", lambda v, qeb=qeb: v.reduce_sum(
            sseg[:, :], qeb[:, :].rearrange("p (h c) -> p h c", h=16), axis=AX.X),
            waits=[t_pq[tau]])
        B.op("V", lambda v: v.reciprocal(rqb[:, :], sseg[:, :]))
        for h in range(16):
            t_qe_done[tau] = B.op("V", lambda v, h=h, tau=tau, qeb=qeb: v.tensor_scalar(
                qsm[:, tau * D + h * 64: tau * D + (h + 1) * 64],
                qeb[:, h * 64:(h + 1) * 64], rqb[:, h:h + 1], 0.125, ALU.mult, ALU.mult),
                inc=("V" if h == 15 else None), nodep=(h > 0))

    for tau in range(3):
        ldtok[tau] = dma_x(tau, xin[tau])
    emit_ln1(0)
    emit_ln1(1)

    for tau in range(NT):
        p = tau % 2
        wt = [t_norm[tau]]
        if tau == 0:
            wt.append(wq_tok[0])
        if tau >= 1:
            wt.append(t_stripe[tau - 1])
        tt = None
        for g in range(8):
            tt = B.op("T", lambda t, g=g, p=p: t.transpose(
                PST[g % 2][:, (g // 2) * P:(g // 2 + 1) * P],
                xhat[p][:, g * P:(g + 1) * P], ident[:, :]),
                waits=(wt if g == 0 else ()), inc=("T" if g == 7 else None))
        t_tr[tau] = tt
        ws = [t_tr[tau]]
        if tau >= 2:
            ws.append(t_strip_cons[tau - 2])
        sview = strip[p][:, :].rearrange("p (q r b) -> p r q b", r=2, b=P)
        B.op("V", lambda v, sview=sview: v.tensor_copy(
            sview[:, 0, :, :], PST[0][:, 0:512].rearrange("p (q b) -> p q b", b=P)),
            waits=ws)
        t_stripe[tau] = B.op("V", lambda v, sview=sview: v.tensor_copy(
            sview[:, 1, :, :], PST[1][:, 0:512].rearrange("p (q b) -> p q b", b=P)),
            inc="V")
        if tau + 2 < NT:
            emit_ln1(tau + 2)
        if tau + 3 < NT:
            ldtok[tau + 3] = dma_x(tau + 3, xin[(tau + 3) % 3], waits=[t_norm[tau]])
        banks6 = [PSQ[0], PSQ[1], PSK[0], PSK[1], PSV[0], PSV[1]]
        wmm = [t_stripe[tau]]
        if tau >= 1:
            wmm += [t_pq[tau - 1], t_pk[tau - 1], t_pv[tau - 1]]
        for c in range(8):
            for j in range(6):
                off = (j // 2) * 1024 + (j % 2) * 512
                wq = list(wmm) if (c == 0 and j == 0) else []
                if tau == 0 and j == 0:
                    wq.append(wq_tok[c])
                B.op("T", lambda t, p=p, c=c, j=j, off=off: t.matmul(
                    banks6[j][:, :], strip[p][:, c * P:(c + 1) * P],
                    wreg[:, c * 3072 + off: c * 3072 + off + 512],
                    start=(c == 0), stop=(c == 7)),
                    waits=wq, inc=("T" if (c == 7 and j == 5) else None))
        tmm = ("T", B.count["T"])
        t_strip_cons[tau] = tmm
        qeb = [qe, qe2][tau % 2]
        wq2 = [tmm]
        if tau >= 2:
            wq2.append(t_qe_done[tau - 2])
        B.op("A", lambda a, qeb=qeb: a.activation(qeb[:, 0:512], PSQ[0][:, :], AF.Exp),
             waits=wq2)
        t_pq[tau] = B.op("A", lambda a, qeb=qeb: a.activation(
            qeb[:, 512:1024], PSQ[1][:, :], AF.Exp), inc="A", nodep=True)
        B.op("A", lambda a, tau=tau: a.activation(
            ktm[:, tau * D: tau * D + 512], PSK[0][:, :], AF.Exp), nodep=True)
        t_pk[tau] = B.op("A", lambda a, tau=tau: a.activation(
            ktm[:, tau * D + 512: (tau + 1) * D], PSK[1][:, :], AF.Exp), inc="A",
            nodep=True)
        B.op("A", lambda a, tau=tau: a.copy(
            vtm[:, tau * D: tau * D + 512], PSV[0][:, :]), nodep=True)
        t_pv[tau] = B.op("A", lambda a, tau=tau: a.copy(
            vtm[:, tau * D + 512: (tau + 1) * D], PSV[1][:, :]), inc="A", nodep=True)
        if tau >= 1:
            emit_softmax(tau - 1)
    emit_softmax(NT - 1)

    t_p12_T = ("T", B.count["T"])
    t_p12_A = ("A", B.count["A"])
    if limit == "p12":
        return _close_early([qsm[:, 0:1024], ktm[:, 0:1024], vtm[:, 0:1024],
                             qsm[:, 15 * 1024:16 * 1024], ktm[:, 15 * 1024:16 * 1024]])

    wout_tok = [dma(wout[:, c * 1024:(c + 1) * 1024], woutT[c * P:(c + 1) * P, :],
                    sem="dWo", waits=([t_p12_T] if c == 0 else ()))
                for c in range(8)]
    wout_tok = [wout_tok[-1]]

    if limit == "pa_only":
        pass
    t_srun0 = B.op("G", lambda g: g.memset(srun[:, :], 0.0), inc="G")

    t_unit_T = {}
    t_unit_V = {}
    for i in range(NT):
        for g in range(NG):
            u = i * NG + g
            par = u % 2
            PSd = [PS0, PS0b][par]
            wt = []
            if u >= 2:
                wt.append(t_unit_V[u - 2])
            if u == 0:
                wt.append(t_p12_A)
            for h in range(2):
                B.op("T", lambda t, i=i, g=g, h=h, PSd=PSd: t.matmul(
                    PSd[h * 64:(h + 1) * 64, 0:64],
                    ktm[:, i * D + g * P + h * 64: i * D + g * P + (h + 1) * 64],
                    vtm[:, i * D + g * P + h * 64: i * D + g * P + (h + 1) * 64],
                    start=True, stop=True), waits=(wt if h == 0 else ()))
            t_unit_T[u] = B.op("T", lambda t, i=i, g=g: t.matmul(
                PS1[:, g * 16 + i: g * 16 + i + 1],
                ktm[:, i * D + g * P: i * D + (g + 1) * P],
                onescol[:, :], start=True, stop=True), inc="T")
            wa = [t_srun0 if u < NG else t_unit_V[u - NG]]
            ta = B.op("A", lambda a, i=i, g=g: a.copy(
                spref[:, g * 1024 + i * 64: g * 1024 + (i + 1) * 64],
                srun[:, g * 64:(g + 1) * 64]), waits=wa, inc="A")
            t_unit_V[u] = B.op("V", lambda v, g=g, PSd=PSd: v.tensor_add(
                srun[:, g * 64:(g + 1) * 64], srun[:, g * 64:(g + 1) * 64],
                PSd[:, 0:64]), waits=[t_unit_T[u], ta], inc="V")

    t_all_units = t_unit_T[NT * NG - 1]
    if limit == "pa":
        return _close_early([spref[:, 0:1024], zpref[:, :], srun[:, :]])
    prev_cz = None
    for g in range(NG):
        wv = [t_all_units]
        if prev_cz is not None:
            wv.append(prev_cz)
        tcz = B.op("V", lambda v, g=g: v.tensor_copy(cz_sb[:, :], PS1[:, g * 16:(g + 1) * 16]),
                   waits=wv, inc="V")
        czbank = PST[g % 2]
        ttr = B.op("T", lambda t, czbank=czbank: t.transpose(
            czbank[0:16, 0:P], cz_sb[:, :], ident[:, :]), waits=[tcz], inc="T")
        tev = B.op("V", lambda v, czbank=czbank: v.tensor_copy(
            czT_sb[:, :], czbank[0:16, 0:P]), waits=[ttr], inc="V")
        tmm = B.op("T", lambda t: t.matmul(PS1b[:, 0:17], czT_sb[:, :], tri16[:, :],
                                           start=True, stop=True), waits=[tev], inc="T")
        prev_cz = B.op("V", lambda v, g=g: v.tensor_scalar_add(
            zpref[:, g * 17:(g + 1) * 17], PS1b[:, 0:17], EPS_Z), waits=[tmm], inc="V")

    B.op("V", lambda v: v.tensor_scalar_mul(blob_s[:, 0:512], srun[:, :], smask[:, 0:1]),
         waits=[prev_cz, t_consts])
    tb2 = B.op("V", lambda v: v.tensor_scalar_mul(
        blob_s[:, 512:520],
        zpref[:, :].rearrange("p (g s) -> p g s", s=17)[:, :, 16], smask[:, 0:1]),
        inc="V")
    tsend = B.op("G", lambda g: g.dma_start(out=cc_in.ap(), in_=blob_s[:, :]),
                 waits=[tb2], inc="dWs")
    B.op("G", lambda g: g.collective_compute(
        "AllReduce", ALU.add, replica_groups=[[0, 1], [2, 3], [4, 5], [6, 7]],
        ins=[cc_in.ap().opt()], outs=[cc_out.ap().opt()]), waits=[tsend], inc="cc")
    trecv = B.op("G", lambda g: g.dma_start(out=brecv[:, :], in_=cc_out.ap()),
                 waits=[("cc", 1), t_p12_T], inc="dWr")

    t_pa_T = ("T", B.count["T"])
    t_pa_V = ("V", B.count["V"])
    t_pa_A = ("A", B.count["A"])
    if limit == "c2":
        return _close_early([spref[:, 0:1024], zpref[:, :], srun[:, :], brecv[:, :]])

    B.op("V", lambda v: v.tensor_scalar_mul(rqb[:, 0:8], brecv[:, 512:520],
                                            rmask[:, 0:1]), waits=[trecv])
    for g in range(NG):
        B.op("V", lambda v, g=g: v.tensor_scalar_add(
            zpref[:, g * 17:(g + 1) * 17], zpref[:, g * 17:(g + 1) * 17],
            rqb[:, g:g + 1]))
    for i in range(NT):
        tok = B.op("V", lambda v, i=i: v.scalar_tensor_tensor(
            spref[:, :].rearrange("p (g i d) -> p g i d", i=NT, d=64)[:, :, i, :],
            brecv[:, 0:512].rearrange("p (g d) -> p g d", d=64), rmask[:, 0:1],
            spref[:, :].rearrange("p (g i d) -> p g i d", i=NT, d=64)[:, :, i, :],
            ALU.mult, ALU.add))
    t_merge = tok

    relZ = [t_pa_A, t_pa_A]
    relO = [t_pa_V, t_pa_V]
    relA_ = t_pa_V
    relI = [t_pa_A, t_pa_A]
    rel_tr_v = t_pa_V
    rel_tr_a = t_pa_A
    rel_qt = [t_pa_T, t_pa_T]
    rel_kc = [t_pa_T, t_pa_T]
    rel_ab = [t_pa_T, t_pa_T]
    rel_rt = [("G", B.count.get("G", 0))] * 2
    rel_tmpO = [t_pa_V, t_pa_V]

    nunits = NT * 4
    if limit.startswith("pbu"):
        nunits = int(limit[3:])

    ta_r_arr = [None, None]
    tv1_a = [None, None]
    ta1_a = [None, None]
    tg2_a = [None, None]
    tt2_a = [None, None]

    def st_tr(u):
        i, gg = u // 4, u % 4
        gA, gB = 2 * gg, 2 * gg + 1
        c0 = (u % 2) * 256
        nonlocal_w = [rel_tr[0], rel_tr[1]]
        B.op("T", lambda t, i=i, gA=gA, c0=c0: t.transpose(
            PS2[:, c0:c0 + P], qsm[:, i * D + gA * P: i * D + (gA + 1) * P],
            ident[:, :]), waits=nonlocal_w)
        B.op("T", lambda t, i=i, gA=gA, c0=c0: t.transpose(
            PS2b[:, c0:c0 + P], ktm[:, i * D + gA * P: i * D + (gA + 1) * P],
            ident[:, :]))
        B.op("T", lambda t, i=i, gB=gB, c0=c0: t.transpose(
            PS2[:, c0 + P:c0 + 256], qsm[:, i * D + gB * P: i * D + (gB + 1) * P],
            ident[:, :]))
        tt2_a[u % 2] = B.op("T", lambda t, i=i, gB=gB, c0=c0: t.transpose(
            PS2b[:, c0 + P:c0 + 256], ktm[:, i * D + gB * P: i * D + (gB + 1) * P],
            ident[:, :]), inc="T")

    def st_evac(u):
        pp = u % 2
        c0 = (u % 2) * 256
        wv1 = [tt2_a[pp], rel_qt[pp]]
        wa1 = [tt2_a[pp], rel_kc[pp]]
        if u == 0:
            wv1.append(t_consts)
            wa1.append(t_consts)
        tv1_a[pp] = B.op("V", lambda v, pp=pp, c0=c0: v.tensor_copy(
            qt[pp][:, :], PS2[:, c0:c0 + 256]), waits=wv1, inc="V")
        ta1_a[pp] = B.op("A", lambda a, pp=pp, c0=c0: a.copy(
            kc[pp][:, :], PS2b[:, c0:c0 + 256]), waits=wa1, inc="A")
        rel_tr[0] = tv1_a[pp]
        rel_tr[1] = ta1_a[pp]

    def st_zr(u, first=False):
        i, gg = u // 4, u % 4
        gA, gB = 2 * gg, 2 * gg + 1
        pz = u % 2
        PSzu = [PS0, PS0b][pz]
        B.op("T", lambda t, PSzu=PSzu, i=i, gA=gA: t.matmul(
            PSzu[:, 0:P], ktm[:, i * D + gA * P: i * D + (gA + 1) * P],
            maskM[:, :], start=True, stop=True), waits=[relZ[pz], relOz[pz]])
        tz = B.op("T", lambda t, PSzu=PSzu, i=i, gB=gB: t.matmul(
            PSzu[:, P:256], ktm[:, i * D + gB * P: i * D + (gB + 1) * P],
            maskM[:, :], start=True, stop=True), inc="T")
        wr = [tz, rel_rt[pz]]
        if first:
            wr.append(t_merge)
        B.op("A", lambda a, pz=pz, PSzu=PSzu, g=gA, i=i: a.activation(
            rt[pz][:, 0:P], PSzu[:, 0:P], AF.Ln,
            bias=zpref[:, g * 17 + i: g * 17 + i + 1]), waits=wr)
        B.op("A", lambda a, pz=pz, PSzu=PSzu, g=gB, i=i: a.activation(
            rt[pz][:, P:256], PSzu[:, P:256], AF.Ln,
            bias=zpref[:, g * 17 + i: g * 17 + i + 1]))
        ta_r_arr[pz] = B.op("A", lambda a, pz=pz: a.activation(
            rt[pz][:, 0:256], rt[pz][:, 0:256], AF.Exp, scale=-1.0), inc="A")
        relZ[pz] = ta_r_arr[pz]

    def st_qmul(u):
        pp = u % 2
        tg2_a[pp] = B.op("G", lambda gp, pp=pp: gp.tensor_mul(
            qt[pp][:, :], qt[pp][:, :], rt[pp][:, :]),
            waits=[tv1_a[pp], ta_r_arr[pp], rel_rt[pp]], inc="G")
        rel_rt[pp] = tg2_a[pp]

    t_xo_last = None

    def st_core(u):
        nonlocal t_xo_last
        global_ = None
        i, gg = u // 4, u % 4
        gA, gB = 2 * gg, 2 * gg + 1
        pp = u % 2
        PSzi = [PS0, PS0b][(u + 1) % 2]
        PSi = [PS3, PS3b][pp]
        c0 = (u % 2) * 256
        B.op("T", lambda t, pp=pp, c0=c0: t.matmul(
            PS1[:, c0:c0 + P], kc[pp][0:64, 0:P], qt[pp][0:64, 0:P],
            start=True, stop=True), waits=[ta1_a[pp], tg2_a[pp], relA_[0]])
        B.op("T", lambda t, pp=pp, c0=c0: t.matmul(
            PS1b[:, c0:c0 + P], kc[pp][64:P, 0:P], qt[pp][64:P, 0:P],
            start=True, stop=True))
        B.op("T", lambda t, pp=pp, c0=c0: t.matmul(
            PS1[:, c0 + P:c0 + 256], kc[pp][0:64, P:256], qt[pp][0:64, P:256],
            start=True, stop=True))
        tA = B.op("T", lambda t, pp=pp, c0=c0: t.matmul(
            PS1b[:, c0 + P:c0 + 256], kc[pp][64:P, P:256], qt[pp][64:P, P:256],
            start=True, stop=True), inc="T")
        rel_kc[pp] = tA
        B.op("V", lambda v, pp=pp, c0=c0: v.tensor_mul(
            ab[pp][:, 0:256], PS1[:, c0:c0 + 256], maskM2[:, :]),
            waits=[tA, rel_ab[pp]])
        tv4 = B.op("V", lambda v, pp=pp, c0=c0: v.tensor_mul(
            ab[pp][:, 256:512], PS1b[:, c0:c0 + 256], maskM2[:, :]), inc="V",
            nodep=True)
        relA_[0] = tv4
        B.op("T", lambda t, pp=pp, PSi=PSi, i=i, gA=gA: t.matmul(
            PSi[0:64, 0:P], vtm[:, i * D + gA * P: i * D + gA * P + 64],
            ab[pp][:, 0:P], start=True, stop=True), waits=[tv4, relI[pp]])
        B.op("T", lambda t, pp=pp, PSi=PSi, i=i, gA=gA: t.matmul(
            PSi[64:P, 0:P], vtm[:, i * D + gA * P + 64: i * D + (gA + 1) * P],
            ab[pp][:, 256:384], start=True, stop=True))
        B.op("T", lambda t, pp=pp, PSi=PSi, i=i, gB=gB: t.matmul(
            PSi[0:64, P:256], vtm[:, i * D + gB * P: i * D + gB * P + 64],
            ab[pp][:, P:256], start=True, stop=True))
        tI = B.op("T", lambda t, pp=pp, PSi=PSi, i=i, gB=gB: t.matmul(
            PSi[64:P, P:256], vtm[:, i * D + gB * P + 64: i * D + (gB + 1) * P],
            ab[pp][:, 384:512], start=True, stop=True), inc="T")
        rel_ab[pp] = tI
        tvo = B.op("V", lambda v, pp=pp, PSi=PSi: v.tensor_copy(
            tmpO[pp][:, :], PSi[:, 0:256]), waits=[tI, rel_tmpO[pp]], inc="V")
        relI[pp] = tvo
        B.op("T", lambda t, PSzi=PSzi, pp=pp, i=i, gA=gA: t.matmul(
            PSzi[0:64, 256:384],
            spref[0:64, gA * 1024 + i * 64: gA * 1024 + (i + 1) * 64],
            qt[pp][0:64, 0:P], start=True, stop=True),
            waits=[tg2_a[pp], relZ[(u + 1) % 2], relOz[(u + 1) % 2]])
        B.op("T", lambda t, PSzi=PSzi, pp=pp, i=i, gA=gA: t.matmul(
            PSzi[64:P, 256:384],
            spref[64:P, gA * 1024 + i * 64: gA * 1024 + (i + 1) * 64],
            qt[pp][64:P, 0:P], start=True, stop=True))
        B.op("T", lambda t, PSzi=PSzi, pp=pp, i=i, gB=gB: t.matmul(
            PSzi[0:64, 384:512],
            spref[0:64, gB * 1024 + i * 64: gB * 1024 + (i + 1) * 64],
            qt[pp][0:64, P:256], start=True, stop=True))
        tJ = B.op("T", lambda t, PSzi=PSzi, pp=pp, i=i, gB=gB: t.matmul(
            PSzi[64:P, 384:512],
            spref[64:P, gB * 1024 + i * 64: gB * 1024 + (i + 1) * 64],
            qt[pp][64:P, P:256], start=True, stop=True), inc="T")
        rel_qt[pp] = tJ
        xodst = xo[:, :].rearrange("p (g t) -> p g t", t=2048)[
            :, gA:gB + 1, i * P:(i + 1) * P]
        t_xo_last = tv5 = B.op("V", lambda v, PSzi=PSzi, pp=pp, xodst=xodst: v.tensor_add(
            xodst, tmpO[pp][:, :].rearrange("p (g t) -> p g t", t=P),
            PSzi[:, 256:512].rearrange("p (g t) -> p g t", t=P)),
            waits=[tJ], inc="V")
        relOz[(u + 1) % 2] = tv5

    relZ = [t_pa_A, t_pa_A]
    relOz = [t_pa_V, t_pa_V]
    relA_ = [t_pa_V]
    relI = [t_pa_A, t_pa_A]
    rel_tr = [t_pa_V, t_pa_A]
    rel_qt = [t_pa_T, t_pa_T]
    rel_kc = [t_pa_T, t_pa_T]
    rel_ab = [t_pa_T, t_pa_T]
    rel_rt = [("G", B.count.get("G", 0))] * 2
    rel_tmpO = [t_pa_V, t_pa_V]

    st_tr(0)
    st_evac(0)
    st_zr(0, first=True)
    if nunits > 1:
        st_zr(1)
    st_qmul(0)
    for u in range(nunits):
        if u + 1 < nunits:
            st_tr(u + 1)
            st_evac(u + 1)
        if u + 2 < nunits:
            st_zr(u + 2)
        if u + 1 < nunits:
            st_qmul(u + 1)
        st_core(u)

    t_pb_T = ("T", B.count["T"])
    t_pb_V = ("V", B.count["V"])
    t_pb_A = ("A", B.count["A"])
    t_pb_G = ("G", B.count["G"])
    if limit == "pb" or limit.startswith("pbu") or limit.startswith("pbo"):
        return _close_early([xo[:, 0:1024], xo[:, 15 * 1024:16 * 1024]])

    dma(lorab_sb[:, :], lraB, sem="dWf", waits=[t_pb_G])
    dma(rw_sb[:, :].rearrange("p (c o) -> p c o", c=8),
        rw_h.rearrange("(c p) o -> p c o", p=P), sem="dWf")
    t_wffn_small = dma(bb_bc[:, :], bb_h.partition_broadcast(P), sem="dWf")
    wbase_tok = [dma(wbase[:, c * 1024:(c + 1) * 1024], baseT[c * P:(c + 1) * P, :],
                     sem="dWb2")
                 for c in range(8)]
    wbase_tok = [wbase_tok[-1]]

    def x2slice(tau):
        if tau < 8:
            return x2a[:, tau * D:(tau + 1) * D]
        return x2b[:, (tau - 8) * D:(tau - 7) * D]

    ld4 = {}
    ld4[0] = dma_x(0, xin[0], waits=[t_pb_V, t_pb_T, t_pb_G, t_pb_A])
    ld4[1] = dma_x(1, xin[1])
    relPQ4 = [t_pb_V, t_pb_V]
    t_v_x2 = {}
    tae4 = {}

    def p4_epilogue(tau):
        par = tau % 2
        cols = colsv[par]
        B.op("V", lambda v, cols=cols: v.tensor_add(cols[:, 0:1], cols[:, 0:1],
                                                    cols[:, 6:7]), waits=[tae4[tau]])
        B.op("V", lambda v, par=par, cols=cols: v.scalar_tensor_tensor(
            junk[:, :], a_sb[par][:, :], 1.0, a_sb[par][:, :], ALU.mult, ALU.mult,
            accum_out=cols[:, 1:2]))
        tb = ln_stats(cols)
        ta2 = B.op("A", lambda a, par=par, cols=cols: a.activation(
            t1[:, :], a_sb[par][:, :], AF.Identity, bias=cols[:, 4:5],
            scale=cols[:, 7:8]), waits=[tb], inc="A")
        wv = [ta2, ld4[tau]]
        if tau == 0:
            wv.append(t_pb_T)
        t_v_x2[tau] = B.op("V", lambda v, tau=tau, par=par: v.tensor_add(
            x2slice(tau), t1[:, :], xin[par][:, :]), waits=wv, inc="V")
        if tau + 2 < NT:
            ld4[tau + 2] = dma_x(tau + 2, xin[par], waits=[t_v_x2[tau]])

    for tau in range(NT):
        par = tau % 2
        PSo = [[PS0, PS0b], [PS1, PS1b]][par]
        cols = colsv[par]
        wt = [relPQ4[par]]
        if tau == 0:
            wt += [t_xo_last] + wout_tok
        for c in range(8):
            for g2 in range(2):
                B.op("T", lambda t, g2=g2, c=c, tau=tau, PSo=PSo: t.matmul(
                    PSo[g2][:, :], xo[:, c * 2048 + tau * P: c * 2048 + (tau + 1) * P],
                    wout[:, c * 1024 + g2 * 512: c * 1024 + (g2 + 1) * 512],
                    start=(c == 0), stop=(c == 7)),
                    waits=(wt if (g2 == 0 and c == 0) else ()),
                    inc=("T" if (g2 == 1 and c == 7) else None))
        tmm = ("T", B.count["T"])
        wa = [tmm]
        if tau >= 2:
            wa.append(t_v_x2[tau - 2])
        B.op("A", lambda a, par=par, PSo=PSo, cols=cols: a.activation(
            a_sb[par][:, 0:512], PSo[0][:, :], AF.Copy, accum_out=cols[:, 0:1]),
            waits=wa)
        tae4[tau] = ta = B.op("A", lambda a, par=par, PSo=PSo, cols=cols: a.activation(
            a_sb[par][:, 512:1024], PSo[1][:, :], AF.Copy, accum_out=cols[:, 6:7]),
            inc="A")
        relPQ4[par] = ta
        if tau >= 1:
            p4_epilogue(tau - 1)
    p4_epilogue(NT - 1)

    t_p4_V = ("V", B.count["V"])
    if limit == "p4":
        return _close_early([x2a[:, 0:1024], x2b[:, 7 * 1024:8 * 1024]])

    relPF = [t_p4_V, t_p4_V]
    relPR = [t_p4_V, t_p4_V]
    t_norm3 = {}
    t_tr3 = {}
    t_stripe3 = {}
    t_stripe3a = {}
    t_ctev = {}
    t_cb = {}
    t_ct_cons = {}
    t_strip_cons3 = {}
    t_x3 = {}
    tff = {}

    t_ln3_tb = {}

    def emit_ln3(tau):
        par = tau % 2
        cols = cols3v[par]
        xsl = x2slice(tau)
        wv = [t_v_x2[tau]]
        B.op("V", lambda v, xsl=xsl, cols=cols: v.reduce_sum(cols[:, 0:1], xsl, axis=AX.X),
             waits=wv)
        wg = [t_v_x2[tau]]
        if tau >= 2:
            wg.append(t_ln3_tb[tau - 2])
        if tau < 2:
            wg.append(t_pb_T)
        tg = B.op("A", lambda a, xsl=xsl, cols=cols: a.activation(
            junk2[:, :], xsl, AF.Square, accum_out=cols[:, 1:2]),
            waits=wg, inc="A")
        tb = ln_stats(cols, w_in=[tg])
        t_ln3_tb[tau] = tb
        wn = [tb]
        if tau >= 2:
            wn.append(t_tr3[tau - 2])
        t_norm3[tau] = B.op("A", lambda a, xsl=xsl, cols=cols, par=par: a.activation(
            xhat[par][:, :], xsl, AF.Identity, bias=cols[:, 4:5], scale=cols[:, 7:8]),
            waits=wn, inc="A")

    def p56_epilogue(tau):
        par = tau % 2
        cols = colsv[par]
        PSf = [[PS0, PS0b], [PS1, PS1b]][par]
        xsl = x2slice(tau)
        wv2 = [tff[tau]]
        if tau >= 2:
            wv2.append(t_x3[tau - 2])
        if tau == 0:
            wv2.append(t_wffn_small)
        B.op("V", lambda v, par=par, PSf=PSf, cols=cols: v.scalar_tensor_tensor(
            a_sb[par][:, 0:512], PSf[0][:, :], 1.0, bb_bc[:, 0:512], ALU.mult, ALU.add,
            accum_out=cols[:, 0:1]), waits=wv2)
        tv3 = B.op("V", lambda v, par=par, PSf=PSf, cols=cols: v.scalar_tensor_tensor(
            a_sb[par][:, 512:1024], PSf[1][:, :], 1.0, bb_bc[:, 512:1024], ALU.mult,
            ALU.add, accum_out=cols[:, 3:4]), inc="V")
        relPF[par] = tv3
        B.op("V", lambda v, cols=cols: v.tensor_add(cols[:, 0:1], cols[:, 0:1],
                                                    cols[:, 3:4]))
        tsq = B.op("A", lambda a, par=par, cols=cols: a.activation(
            junk[:, :], a_sb[par][:, :], AF.Square, accum_out=cols[:, 1:2]),
            waits=[tv3], inc="A")
        tb2 = ln_stats(cols, w_in=[tsq])
        ta2 = B.op("A", lambda a, par=par, cols=cols: a.activation(
            t1[:, :], a_sb[par][:, :], AF.Identity, bias=cols[:, 4:5],
            scale=cols[:, 7:8]), waits=[tb2], inc="A")
        t_x3[tau] = B.op("V", lambda v, xsl=xsl: v.tensor_add(xsl, xsl, t1[:, :]),
                         waits=[ta2], inc="V")
        B.op("S", lambda s, tau=tau, xsl=xsl: s.dma_start(
            out=o_hbm[tau * P:(tau + 1) * P, :], in_=xsl), waits=[t_x3[tau]],
            inc="dOut")

    emit_ln3(0)
    emit_ln3(1)
    for tau in range(NT):
        par = tau % 2
        cols = colsv[par]
        PSf = [[PS0, PS0b], [PS1, PS1b]][par]
        PSr = [PS3, PS3b][par]
        PSt = [PS2, PS2b][par]
        wt = [t_norm3[tau]]
        if tau >= 1:
            wt.append(t_stripe3[tau - 1])
            wt.append(t_stripe3a[tau - 1])
            wt.append(t_ctev[tau - 1])
        else:
            wt += [t_pb_V, t_pb_A]
        tt = None
        for g in range(8):
            tt = B.op("T", lambda t, g=g, par=par, tau=tau: t.transpose(
                PST[(g + tau) % 2][:, (g // 2) * P:(g // 2 + 1) * P],
                xhat[par][:, g * P:(g + 1) * P], ident[:, :]),
                waits=(wt if g == 0 else ()), inc=("T" if g == 7 else None))
        t_tr3[tau] = tt
        ws = [t_tr3[tau]]
        if tau >= 2:
            ws.append(t_strip_cons3[tau - 2])
        else:
            ws.append(t_pb_G)
        sview = strip[par][:, :].rearrange("p (q r b) -> p r q b", r=2, b=P)
        t_stripe3a[tau] = tsa = B.op("A", lambda a, sview=sview, tau=tau: a.copy(
            sview[:, 1 - tau % 2, :, :],
            PST[1][:, 0:512].rearrange("p (q b) -> p q b", b=P)), waits=ws, inc="A")
        t_stripe3[tau] = B.op("V", lambda v, sview=sview, tau=tau: v.tensor_copy(
            sview[:, tau % 2, :, :], PST[0][:, 0:512].rearrange("p (q b) -> p q b", b=P)),
            waits=ws + [0 and tsa] if False else ws, inc="V")
        wr = [t_stripe3[tau], t_stripe3a[tau],
              (t_cb[tau - 2] if tau >= 2 else relPR[par])]
        if tau == 0:
            wr.append(t_wffn_small)
        for c in range(8):
            B.op("T", lambda t, c=c, par=par, PSr=PSr: t.matmul(
                PSr[:, 0:16], strip[par][:, c * P:(c + 1) * P],
                rw_sb[:, c * 16:(c + 1) * 16], start=(c == 0), stop=(c == 7)),
                waits=(wr if c == 0 else ()), inc=("T" if c == 7 else None))
        tr = ("T", B.count["T"])
        wb = [relPF[par]]
        if tau == 0:
            wb += wbase_tok
        for c in range(8):
            for g2 in range(2):
                t_base = B.op("T", lambda t, g2=g2, c=c, par=par, PSf=PSf: t.matmul(
                    PSf[g2][:, :], strip[par][:, c * P:(c + 1) * P],
                    wbase[:, c * 1024 + g2 * 512: c * 1024 + (g2 + 1) * 512],
                    start=(c == 0), stop=False),
                    waits=(wb if (g2 == 0 and c == 0) else ()),
                    inc=("T" if (c == 7 and g2 == 1) else None))
        t_strip_cons3[tau] = t_base
        wlg = [tr]
        if tau == 0:
            wlg.append(t_consts)
        tv = B.op("V", lambda v, PSr=PSr: v.tensor_add(lgb[:, :], PSr[:, 0:8], rb_bc[:, :]),
                  waits=wlg, inc="V")
        ta = B.op("A", lambda a: a.activation(geb[:, :], lgb[:, :], AF.Exp,
                                              accum_out=gsb[:, :]), waits=[tv], inc="A")
        B.op("V", lambda v: v.reciprocal(gqb[:, :], gsb[:, :]), waits=[ta])
        tv2 = B.op("V", lambda v: v.tensor_scalar_mul(gbuf[:, :], geb[:, :], gqb[:, 0:1]))
        tv2 = B.op("V", lambda v: v.tensor_copy(gbuf32[:, :], gbuf[:, :]), inc="V")
        wc = [tv2]
        if tau >= 2:
            wc.append(t_ct_cons[tau - 2])
        tcb = None
        for e in range(8):
            tcb = B.op("V", lambda v, e=e, par=par, PSr=PSr: v.tensor_scalar(
                cbuf[par][:, e * 8:(e + 1) * 8], PSr[:, 8:16], gbuf32[:, e:e + 1], 2.0,
                ALU.mult, ALU.mult), waits=(wc if e == 0 else ()),
                inc=("V" if e == 7 else None), nodep=(e > 0))
        t_cb[tau] = tcb
        cslot = PST[(tau + 1) % 2][0:64, 512:512 + P]
        wct = [tcb]
        if tau >= 2:
            wct.append(t_ctev[tau - 2])
        tct = B.op("T", lambda t, cslot=cslot, par=par: t.transpose(
            cslot, cbuf[par][:, :], ident[:, :]), waits=wct, inc="T")
        t_ctev[tau] = tce = B.op("V", lambda v, cslot=cslot, par=par: v.tensor_copy(
            ctsb[par][:, :], cslot), waits=[tct], inc="V")
        for g2 in range(2):
            tff[tau] = B.op("T", lambda t, g2=g2, par=par, PSf=PSf: t.matmul(
                PSf[g2][:, :], ctsb[par][:, :],
                lorab_sb[:, g2 * 512:(g2 + 1) * 512], start=False, stop=True),
                waits=([tce] if g2 == 0 else ()),
                inc=("T" if g2 == 1 else None))
        t_ct_cons[tau] = tff[tau]
        relPR[par] = t_cb[tau]
        if tau + 2 < NT:
            emit_ln3(tau + 2)
        if tau >= 1:
            p56_epilogue(tau - 1)
    p56_epilogue(NT - 1)

    B.op("S", lambda s: s.nop(), waits=[("dOut", 16 * NT)])

    from contextlib import ExitStack
    sem_names = sorted(DMA_SEMS) + ["cc", "T", "V", "A", "G"]
    with ExitStack() as stack:
        sems = {n: stack.enter_context(nc.semaphore(f"sem_{n}")) for n in sem_names}
        blk = stack.enter_context(nc.Block())
        B.replay(blk, sems)
    return nc


_NC_CACHE = None


def kernel(L0_attn_x, L0_attn_W_qkv, L0_attn_W_out, L0_ffn_base_w, L0_ffn_base_b,
           L0_ffn_lora_A, L0_ffn_lora_B, L0_ffn_router_w, L0_ffn_router_b):
    global _NC_CACHE
    x = np.ascontiguousarray(np.asarray(L0_attn_x, dtype=np.float32))
    wqkvT = np.ascontiguousarray(np.asarray(L0_attn_W_qkv, dtype=np.float32).T)
    woutT = np.ascontiguousarray(np.asarray(L0_attn_W_out, dtype=np.float32).T)
    baseT = np.ascontiguousarray(np.asarray(L0_ffn_base_w, dtype=np.float32).T)
    rw = np.ascontiguousarray(np.concatenate(
        [np.asarray(L0_ffn_router_w, dtype=np.float32).T,
         np.asarray(L0_ffn_lora_A, dtype=np.float32)], axis=1))
    lorab = np.ascontiguousarray(np.asarray(L0_ffn_lora_B, dtype=np.float32).reshape(64, 1024))
    rb = np.ascontiguousarray(np.asarray(L0_ffn_router_b, dtype=np.float32).reshape(1, 8))
    bb = np.ascontiguousarray(np.asarray(L0_ffn_base_b, dtype=np.float32).reshape(1, 1024))

    ident = np.eye(P, dtype=np.float32)
    s_idx = np.arange(P)[:, None]
    t_idx = np.arange(P)[None, :]
    maskM = (s_idx <= t_idx).astype(np.float32)
    maskM2 = np.concatenate([maskM, maskM], axis=1)
    tri16 = np.zeros((16, 17), dtype=np.float32)
    for j in range(16):
        tri16[j, j + 1:16] = 1.0
        tri16[j, 16] = 1.0
    ones_col = np.ones((P, 1), dtype=np.float32)
    EPS_LN_NP = 1e-5

    if _NC_CACHE is None:
        _NC_CACHE = build_nc()
    nc = _NC_CACHE

    in_maps = []
    for c in range(8):
        b, h = c // 2, c % 2
        send = 1.0 if h == 0 else 0.0
        in_maps.append({
            "x": np.ascontiguousarray(x[b, h * 2048:(h + 1) * 2048, :]),
            "wqkvT": wqkvT, "woutT": woutT, "baseT": baseT, "rw": rw,
            "loraB": lorab, "rb": rb, "bb": bb,
            "ident": ident, "maskM": maskM, "maskM2": maskM2, "tri16": tri16,
            "onescol": ones_col,
            "cvals": np.broadcast_to(np.array([EPS_LN_NP, -0.5], dtype=np.float32), (P, 2)).copy(),
            "smask": np.full((P, 1), send, dtype=np.float32),
            "rmask": np.full((P, 1), 1.0 - send, dtype=np.float32),
        })
    res = run_bass_kernel_spmd(nc, in_maps, core_ids=list(range(8)))
    out = np.empty((4, 4096, 1024), dtype=np.float32)
    for c in range(8):
        b, h = c // 2, c % 2
        out[b, h * 2048:(h + 1) * 2048, :] = res.results[c]["out"]
    return out
